# version 04
# baseline (speedup 1.0000x reference)
import math
from contextlib import ExitStack

import numpy as np
import ml_dtypes

import concourse.bass as bass
import concourse.mybir as mybir
from concourse.bass_utils import run_bass_kernel_spmd

F32 = mybir.dt.float32
U8 = mybir.dt.uint8
BF16 = mybir.dt.bfloat16
FP16 = mybir.dt.float16
Act = mybir.ActivationFunctionType
Alu = mybir.AluOpType
AX = mybir.AxisListType.X

ATTN_STD = [1.0, 2.0, 4.0, 8.0, 1.0, 2.0, 4.0, 8.0]
ATTN_OFF = [-1.0, -2.0, -4.0, -8.0, -1.0, -2.0, -4.0, -8.0]


class Cfg:
    def __init__(self, nch=32, nich=32, bisect=36, stop_after=None):
        self.stop_after = stop_after
        self.D = 1024
        self.DC = 8
        self.NCH = nch
        self.S = 128 * nch
        self.NICH = nich
        self.I = 128 * nich
        self.E = 8
        self.K = self.S // self.E * 2
        self.RC = self.K // 128
        self.RH = self.K // 2
        self.NH = 8
        self.BISECT = bisect


class Sem:
    def __init__(self, es, nc, name):
        self.h = es.enter_context(nc.semaphore(name))
        self.v = 0

    def inc(self, instr, n=1):
        instr.then_inc(self.h, n)
        self.v += n
        return self.v


class DmaQ:
    def __init__(self, es, nc, name, n):
        self.sems = [Sem(es, nc, f"{name}{i}") for i in range(n)]
        self.i = 0

    def dma(self, eng, out, in_):
        s = self.sems[self.i % len(self.sems)]
        eng.wait_ge(s.h, s.v)
        it = eng.dma_start(out=out, in_=in_)
        s.inc(it, 16)
        self.i += 1
        return (s, s.v)

    @staticmethod
    def wait(eng, mark):
        if mark is None:
            return
        if isinstance(mark, list):
            for m in mark:
                eng.wait_ge(m[0].h, m[1])
        else:
            eng.wait_ge(mark[0].h, mark[1])

    def snapshot(self):
        return [(s, s.v) for s in self.sems]

    def wait_all(self, eng):
        for s in self.sems:
            eng.wait_ge(s.h, s.v)


def build_program(cfg: Cfg):
    nc = bass.Bass("TRN2", target_bir_lowering=False)
    D, DC, NCH, E, K, RC, RH = cfg.D, cfg.DC, cfg.NCH, cfg.E, cfg.K, cfg.RC, cfg.RH
    S, I, NICH, NH = cfg.S, cfg.I, cfg.NICH, cfg.NH

    x_in = nc.dram_tensor("x_in", [S, D], F32, kind="ExternalInput").ap()
    wgT = nc.dram_tensor("wgT", [E, D, I], F32, kind="ExternalInput").ap()
    wuT = nc.dram_tensor("wuT", [E, D, I], F32, kind="ExternalInput").ap()
    wdT = nc.dram_tensor("wdT", [E, I, D], F32, kind="ExternalInput").ap()
    gwT = nc.dram_tensor("gwT", [D, E], F32, kind="ExternalInput").ap()
    wiT = nc.dram_tensor("wiT", [D, D], F32, kind="ExternalInput").ap()
    woT = nc.dram_tensor("woT", [D, D], F32, kind="ExternalInput").ap()
    out = nc.dram_tensor("out", [S, D], F32, kind="ExternalOutput").ap()
    x_state = nc.dram_tensor("x_state", [S, D], F32, kind="Internal").ap()
    wg_bf = nc.dram_tensor("wg_bf", [E, D, I], BF16, kind="Internal").ap()
    wu_bf = nc.dram_tensor("wu_bf", [E, D, I], BF16, kind="Internal").ap()
    wd_bf = nc.dram_tensor("wd_bf", [E, I, D], BF16, kind="Internal").ap()
    wi_bf = nc.dram_tensor("wi_bf", [D, D], BF16, kind="Internal").ap()
    wo_bf = nc.dram_tensor("wo_bf", [D, D], BF16, kind="Internal").ap()
    gw_bf = nc.dram_tensor("gw_bf", [D, E], BF16, kind="Internal").ap()

    es = ExitStack()
    sb = lambda name, shape, dt: es.enter_context(nc.sbuf_tensor(name, shape, dt))

    with es:
        xln = sb("xln", [128, NCH * 1024], BF16)
        tsl = sb("tsl", [128, 2 * 1024], BF16)
        xch = sb("xch", [128, 2 * 1024], F32)
        a_cols = sb("a_cols", [128, NCH * E], F32)
        a_cols_bf = sb("a_cols_bf", [128, NCH * E], BF16)
        ct16 = sb("ct16", [8, S], FP16)
        ct_cols = sb("ct_cols", [128, NCH * E], F32)
        iota_r = sb("iota_r", [128, 1024], FP16)
        iota_pr = sb("iota_pr", [128, RC], F32)
        ident_b = sb("ident_b", [128, 128], BF16)
        ident_f = sb("ident_f", [128, 128], F32)
        ones16 = sb("ones16", [1, 128], FP16)
        e_sel = sb("e_sel", [8, 8 * 128], FP16)
        gw_sb = sb("gw_sb", [128, DC * E], BF16)
        g_cols = sb("g_cols", [128, 2 * RC], F32)
        g_row = sb("g_row", [1, K], F32)
        mneg = sb("mneg", [128, 1], F32)
        lnt = sb("lnt", [128, 1], F32)
        vsum = sb("vsum", [128, 1], F32)
        rstd = sb("rstd", [128, 1], F32)
        negmx = sb("negmx", [128, 1], F32)
        ssum = sb("ssum", [128, 1], F32)
        srcp = sb("srcp", [128, 1], F32)
        aexp = sb("aexp", [128, E], F32)
        sil_t = sb("sil_t", [128, 2 * 512], F32)
        sq_junk = sil_t[:].bitcast(BF16)[:, 0:1024]
        const_sb = sb("const_sb", [128, 2], F32)
        nc.const_aps.register(F32, 0.0, const_sb[:, 0:1])
        nc.const_aps.register(F32, 1e-5, const_sb[:, 1:2])

        dqx = DmaQ(es, nc, "dqx", 8)
        dqg = DmaQ(es, nc, "dqg", 2)
        sm_gc = Sem(es, nc, "sm_gc")
        sm_t = Sem(es, nc, "sm_t")
        sm_v = Sem(es, nc, "sm_v")
        sm_s = Sem(es, nc, "sm_s")
        block = es.enter_context(nc.Block())
        ps = es.enter_context(nc.psum_tensor("ps", [128, 4096], F32))
        psb = ps[:].bitcast(BF16)

        xstate_mark = [None] * NCH
        marks = {}
        dist0 = xch[:, 0:384]
        prf = xch[:, 384:768]
        e_sel_i = xch[0:8, 768:1792]

        def barrier():
            tv, vv, sv = sm_t.v, sm_v.v, sm_s.v

            @block.tensor
            def _(tn):
                tn.wait_ge(sm_v.h, vv)
                tn.wait_ge(sm_s.h, sv)

            @block.vector
            def _(v):
                v.wait_ge(sm_t.h, tv)
                v.wait_ge(sm_s.h, sv)
                v.drain()

            @block.scalar
            def _(s):
                s.wait_ge(sm_t.h, tv)
                s.wait_ge(sm_v.h, vv)
                s.drain()

        @block.gpsimd
        def _(g):
            it = g.iota(iota_r[:], pattern=[[1, 1024]], base=0, channel_multiplier=0,
                        allow_small_or_imprecise_dtypes=True)
            it = g.iota(iota_pr[:], pattern=[[128, RC]], base=0, channel_multiplier=1,
                        allow_small_or_imprecise_dtypes=True)
            it = g.iota(prf[:, 0:128], pattern=[[1, 128]], base=0,
                        channel_multiplier=-1, allow_small_or_imprecise_dtypes=True)
            it = g.iota(e_sel_i, pattern=[[-1, 8], [0, 128]], base=0,
                        channel_multiplier=1, allow_small_or_imprecise_dtypes=True)
            it = g.memset(ones16[:], 1.0)
            sm_gc.inc(it, 5)
            marks["gw_cast"] = dqg.dma(g, gw_bf, gwT)
            marks["wi_cast"] = dqg.dma(g, wi_bf, wiT)
            marks["wo_cast"] = dqg.dma(g, wo_bf, woT)

        cast_done = {}

        @block.gpsimd
        def _(g):
            for e in range(E):
                dqg.dma(g, wg_bf[e], wgT[e])
                dqg.dma(g, wu_bf[e], wuT[e])
                dqg.dma(g, wd_bf[e], wdT[e])
                cast_done[e] = dqg.snapshot()

        @block.vector
        def _(v):
            v.memset(const_sb[:, 0:1], 0.0)
            v.memset(const_sb[:, 1:2], 1e-5)
            v.drain()
            v.wait_ge(sm_gc.h, 5)
            it = v.tensor_scalar(e_sel[:], e_sel_i, 0.0, None, Alu.is_equal)
            it = v.tensor_scalar(ident_b[:], prf[:, 0:128], 0.0, None, Alu.is_equal)
            it = v.tensor_scalar(ident_f[:], prf[:, 0:128], 0.0, None, Alu.is_equal)
            sm_v.inc(it)

        marks["const_vm"] = sm_v.v

        @block.sync
        def _(s):
            DmaQ.wait(s, marks["gw_cast"])
            marks["gw_sb"] = dqx.dma(s, gw_sb[:],
                                     gw_bf.rearrange("(c p) e -> p c e", p=128))

        def emit_ln_chunk(src_ap, dst_ap):
            @block.vector
            def _(v):
                v.drain()
                it = v.tensor_reduce(mneg[:], src_ap, AX, Alu.add, negate=True)
                v.drain()
                it = v.tensor_scalar(lnt[:], mneg[:], 1.0 / D, None, Alu.mult)
                v.drain()
                it = v.tensor_scalar(src_ap, src_ap, lnt[:, 0:1], None, Alu.add)
                v.drain()
                it = v.scalar_tensor_tensor(sq_junk, src_ap, 1.0, src_ap,
                                            Alu.mult, Alu.mult, accum_out=vsum[:])
                sm_v.inc(it)
            m1 = sm_v.v

            @block.scalar
            def _(s):
                s.wait_ge(sm_v.h, m1)
                it = s.activation(rstd[:], vsum[:], Act.Ln, bias=1e-5, scale=1.0 / D)
                s.drain()
                it = s.activation(rstd[:], rstd[:], Act.Exp, bias=0.0, scale=-0.5)
                sm_s.inc(it)
            m2 = sm_s.v

            @block.vector
            def _(v):
                v.wait_ge(sm_s.h, m2)
                it = v.tensor_scalar(dst_ap, src_ap, rstd[:, 0:1], None, Alu.mult)
                sm_v.inc(it)
            return sm_v.v

        def phase_ln_router(call, src_dram, aff_T):
            slab_ev = []
            router_m = []
            smx_cons = []
            ln_done = []
            aT_ev = []
            for c in range(NCH):
                buf = c % 2
                if src_dram is not None:
                    xm_box = []

                    @block.sync
                    def _(s):
                        if c < 2:
                            s.wait_ge(sm_v.h, marks["const_vm"])
                        if c >= 2:
                            s.wait_ge(sm_v.h, ln_done[c - 2])
                        xm_box.append(
                            dqx.dma(s, xch[:, buf * 1024:(buf + 1) * 1024],
                                    src_dram[c * 128:(c + 1) * 128, :]))

                    @block.vector
                    def _(v):
                        DmaQ.wait(v, xm_box[0])
                    ln_done.append(
                        emit_ln_chunk(xch[:, buf * 1024:(buf + 1) * 1024],
                                      xln[:, c * 1024:(c + 1) * 1024]))
                lnm = sm_v.v

                @block.tensor
                def _(tn):
                    tn.wait_ge(sm_v.h, lnm)
                    if slab_ev:
                        tn.wait_ge(sm_v.h, slab_ev[-1])
                    for dcc in range(DC):
                        it = tn.transpose(psb[:, dcc * 128:(dcc + 1) * 128],
                                          xln[:, c * 1024 + dcc * 128:
                                              c * 1024 + (dcc + 1) * 128],
                                          ident_b[:])
                    sm_t.inc(it)
                ttm = sm_t.v

                @block.vector
                def _(v):
                    v.wait_ge(sm_t.h, ttm)
                    if len(router_m) >= 2:
                        v.wait_ge(sm_t.h, router_m[-2])
                    it = v.tensor_copy(tsl[:, buf * 1024:(buf + 1) * 1024],
                                       psb[:, 0:1024])
                    sm_v.inc(it)
                slab_ev.append(sm_v.v)
                evm = sm_v.v

                @block.tensor
                def _(tn):
                    tn.wait_ge(sm_v.h, evm)
                    DmaQ.wait(tn, marks["gw_sb"])
                    if len(smx_cons) >= 2:
                        tn.wait_ge(sm_s.h, smx_cons[-2][0])
                        tn.wait_ge(sm_v.h, smx_cons[-2][1])
                    rbase = 1024 + (c % 2) * 512
                    for dcc in range(DC):
                        it = tn.matmul(ps[:, rbase:rbase + E],
                                       tsl[:, buf * 1024 + dcc * 128:
                                           buf * 1024 + (dcc + 1) * 128],
                                       gw_sb[:, dcc * E:(dcc + 1) * E],
                                       start=(dcc == 0), stop=(dcc == DC - 1))
                    sm_t.inc(it)
                router_m.append(sm_t.v)
                rm = sm_t.v

                @block.vector
                def _(v):
                    v.wait_ge(sm_t.h, rm)
                    rbase = 1024 + (c % 2) * 512
                    it = v.reduce_max(negmx[:], ps[:, rbase:rbase + E], AX,
                                      negate=True)
                    sm_v.inc(it)
                nm = sm_v.v

                @block.scalar
                def _(s):
                    s.wait_ge(sm_v.h, nm)
                    rbase = 1024 + (c % 2) * 512
                    it = s.activation(aexp[:], ps[:, rbase:rbase + E], Act.Exp,
                                      bias=negmx[:, 0:1], scale=1.0)
                    sm_s.inc(it)
                sx = sm_s.v

                @block.vector
                def _(v):
                    v.wait_ge(sm_s.h, sx)
                    it = v.tensor_reduce(ssum[:], aexp[:], AX, Alu.add)
                    v.drain()
                    it = v.reciprocal(srcp[:], ssum[:])
                    v.drain()
                    it = v.tensor_scalar(a_cols[:, c * E:(c + 1) * E], aexp[:],
                                         srcp[:, 0:1], 0.5, Alu.mult, Alu.mult)
                    sm_v.inc(it)
                smx_cons.append((sx, sm_v.v))
                am = sm_v.v

                @block.tensor
                def _(tn):
                    tn.wait_ge(sm_v.h, am)
                    if aT_ev:
                        tn.wait_ge(sm_v.h, aT_ev[-1])
                    it = tn.transpose(ps[0:E, 2048:2048 + 128],
                                      a_cols[:, c * E:(c + 1) * E], ident_f[:])
                    sm_t.inc(it)
                t2 = sm_t.v

                @block.vector
                def _(v):
                    v.wait_ge(sm_t.h, t2)
                    it = v.tensor_copy(aff_T[:, c * 128:(c + 1) * 128],
                                       ps[0:E, 2048:2048 + 128])
                    sm_v.inc(it)
                aT_ev.append(sm_v.v)

        def phase_selection(call, sel):
            aff_T, mask_T, ct32, bis_junk, lo_t, hi_t, mid_t, cnt_t, ge_t, gi_t = sel

            @block.vector
            def _(v):
                v.drain()
                it = v.tensor_copy(a_cols_bf[:], a_cols[:])
                v.drain()
                it = v.memset(lo_t[:], 0.0)
                it = v.memset(hi_t[:], 1.0)
                it = v.memset(mid_t[:], 0.5)
                v.drain()
                for _i in range(cfg.BISECT):
                    it = v.tensor_scalar(bis_junk[:], aff_T[:], mid_t[:, 0:1], None,
                                         Alu.is_ge, Alu.add, accum_out=cnt_t[:])
                    v.drain()
                    it = v.tensor_scalar(ge_t[:], cnt_t[:], float(K) - 0.5, None,
                                         Alu.is_ge)
                    it = v.tensor_scalar(gi_t[:], cnt_t[:], float(K) - 0.5, None,
                                         Alu.is_lt)
                    v.drain()
                    it = v.select(lo_t[:], ge_t[:], mid_t[:], lo_t[:],
                                  add_drain=True)
                    v.drain()
                    it = v.select(hi_t[:], gi_t[:], mid_t[:], hi_t[:],
                                  add_drain=True)
                    v.drain()
                    it = v.tensor_scalar(mid_t[:], lo_t[:], hi_t[:, 0:1], 0.5,
                                         Alu.add, Alu.mult)
                    v.drain()
                it = v.tensor_scalar(mask_T[:], aff_T[:], lo_t[:, 0:1], None,
                                     Alu.is_ge)
                v.drain()
                it = v.tensor_tensor_scan(ct32[:], mask_T[:], mask_T[:], 0.0,
                                          Alu.add, Alu.bypass)
                v.drain()
                it = v.tensor_mul(ct32[:], ct32[:], mask_T[:])
                v.drain()
                it = v.tensor_scalar(ct32[:], ct32[:], 1.0, None, Alu.subtract)
                v.drain()
                it = v.tensor_copy(ct16[:], ct32[:])
                sm_v.inc(it)
            selm = sm_v.v

            prev = []
            for c in range(NCH):
                @block.tensor
                def _(tn):
                    tn.wait_ge(sm_v.h, selm)
                    if len(prev) >= 2:
                        tn.wait_ge(sm_v.h, prev[-2])
                    pofs = 2048 + (c % 2) * 512
                    it = tn.transpose(ps[0:128, pofs:pofs + E],
                                      ct32[:, c * 128:(c + 1) * 128],
                                      ident_f[0:E, 0:E])
                    sm_t.inc(it)
                tm = sm_t.v

                @block.vector
                def _(v):
                    v.wait_ge(sm_t.h, tm)
                    pofs = 2048 + (c % 2) * 512
                    it = v.tensor_copy(ct_cols[:, c * E:(c + 1) * E],
                                       ps[0:128, pofs:pofs + E])
                    sm_v.inc(it)
                prev.append(sm_v.v)
            return sm_v.v

        def phase_expert(call, e, slot, st, sel_done):
            x_inT, y_pair, PT_b, hT, wgu_sl, wd_sl = (
                st["x_inT"], st["y_pair"], st["PT_b"], st["hT"], st["wgu_sl"],
                st["wd_sl"])
            pt_cons = st["pt_cons"]

            for h in range(2):
                for grp in range(2):
                    for c in range(NCH):
                        pb = c % 2

                        @block.vector
                        def _(v):
                            if c == 0 and h == 0 and grp == 0:
                                v.wait_ge(sm_v.h, sel_done)
                                v.drain()
                            if len(pt_cons) >= 2:
                                v.wait_ge(sm_t.h, pt_cons[-2])
                            it = v.tensor_scalar(
                                PT_b[:, pb * RH:(pb + 1) * RH],
                                iota_r[:, h * RH:h * RH + RH],
                                ct_cols[:, c * E + e:c * E + e + 1],
                                None, Alu.is_equal)
                            sm_v.inc(it)
                        vm = sm_v.v

                        @block.tensor
                        def _(tn):
                            tn.wait_ge(sm_v.h, vm)
                            if c == 0:
                                if st["gath_ev"]:
                                    tn.wait_ge(sm_v.h, st["gath_ev"][-1])
                                if st["down_ev"]:
                                    tn.wait_ge(sm_v.h, st["down_ev"][-1])
                            for dd in range(4):
                                dcc = grp * 4 + dd
                                it = tn.matmul(
                                    ps[:, dd * 512:dd * 512 + RH],
                                    xln[:, c * 1024 + dcc * 128:
                                        c * 1024 + (dcc + 1) * 128],
                                    PT_b[:, pb * RH:(pb + 1) * RH],
                                    start=(c == 0), stop=(c == NCH - 1))
                            if grp == 0:
                                it = tn.matmul(
                                    ps[0:1, 2048:2048 + RH],
                                    a_cols_bf[:, c * E + e:c * E + e + 1],
                                    PT_b[:, pb * RH:(pb + 1) * RH],
                                    start=(c == 0), stop=(c == NCH - 1))
                            sm_t.inc(it)
                        pt_cons.append(sm_t.v)
                    gm = sm_t.v

                    @block.vector
                    def _(v):
                        v.wait_ge(sm_t.h, gm)
                        for dd in range(4):
                            dcc = grp * 4 + dd
                            it = v.tensor_copy(
                                x_inT[:, dcc * K + h * RH:dcc * K + h * RH + RH],
                                ps[:, dd * 512:dd * 512 + RH])
                        if grp == 0:
                            it = v.tensor_copy(g_row[0:1, h * RH:h * RH + RH],
                                               ps[0:1, 2048:2048 + RH])
                        sm_v.inc(it)
                    st["gath_ev"].append(sm_v.v)
            xin_ready = sm_v.v

            prev = []
            for j in range(RC):
                @block.tensor
                def _(tn):
                    tn.wait_ge(sm_v.h, xin_ready)
                    if len(prev) >= 2:
                        tn.wait_ge(sm_v.h, prev[-2])
                    pofs = 2560 + (j % 2)
                    it = tn.transpose(ps[:, pofs:pofs + 1],
                                      g_row[0:1, j * 128:(j + 1) * 128],
                                      ident_f[0:1, 0:1])
                    sm_t.inc(it)
                tj = sm_t.v

                @block.vector
                def _(v):
                    v.wait_ge(sm_t.h, tj)
                    pofs = 2560 + (j % 2)
                    it = v.tensor_copy(g_cols[:, slot * RC + j:slot * RC + j + 1],
                                       ps[:, pofs:pofs + 1])
                    sm_v.inc(it)
                prev.append(sm_v.v)
            g_ready = sm_v.v

            for h in range(2):
                gu_ev = []
                for ic in range(NICH):
                    if ic % 2 == 0:
                        @block.sync
                        def _(s):
                            DmaQ.wait(s, cast_done[e])
                            if len(st["wgu_marks"]) >= 3:
                                s.wait_ge(sm_t.h, st["wgu_marks"][-3])
                            i0 = ic * 128
                            bofs = (ic // 2 % 2) * 2048
                            m1_ = dqx.dma(s, wgu_sl[:, bofs:bofs + 2048],
                                          wg_bf[e][:, i0:i0 + 256].rearrange(
                                              "(dc p) i -> p dc i", p=128))
                            m2_ = dqx.dma(
                                s, wgu_sl[:, 4096 + bofs:4096 + bofs + 2048],
                                wu_bf[e][:, i0:i0 + 256].rearrange(
                                    "(dc p) i -> p dc i", p=128))
                            st["wgu_dma"].append([m1_, m2_])
                    wm = st["wgu_dma"][-1]

                    @block.tensor
                    def _(tn):
                        DmaQ.wait(tn, wm)
                        if ic == 0:
                            tn.wait_ge(sm_v.h, xin_ready)
                            if st["down_ev"]:
                                tn.wait_ge(sm_v.h, st["down_ev"][-1])
                        if len(gu_ev) >= 2:
                            tn.wait_ge(sm_v.h, gu_ev[-2])
                        base = (ic // 2 % 2) * 2048 + (ic % 2) * 128
                        pG = (ic % 2) * 512
                        pU = 1024 + (ic % 2) * 512
                        for dcc in range(DC):
                            it = tn.matmul(
                                ps[:, pG:pG + RH],
                                wgu_sl[:, base + dcc * 256:base + dcc * 256 + 128],
                                x_inT[:, dcc * K + h * RH:dcc * K + h * RH + RH],
                                start=(dcc == 0), stop=(dcc == DC - 1))
                            it = tn.matmul(
                                ps[:, pU:pU + RH],
                                wgu_sl[:, 4096 + base + dcc * 256:
                                       4096 + base + dcc * 256 + 128],
                                x_inT[:, dcc * K + h * RH:dcc * K + h * RH + RH],
                                start=(dcc == 0), stop=(dcc == DC - 1))
                        sm_t.inc(it)
                    st["wgu_marks"].append(sm_t.v)
                    gum = sm_t.v

                    @block.scalar
                    def _(s):
                        s.wait_ge(sm_t.h, gum)
                        if len(gu_ev) >= 2:
                            s.wait_ge(sm_v.h, gu_ev[-2])
                        pG = (ic % 2) * 512
                        so = (ic % 2) * 512
                        it = s.activation(sil_t[:, so:so + RH], ps[:, pG:pG + RH],
                                          Act.Silu, bias=0.0, scale=1.0)
                        sm_s.inc(it)
                    sm2 = sm_s.v

                    @block.vector
                    def _(v):
                        v.wait_ge(sm_s.h, sm2)
                        pU = 1024 + (ic % 2) * 512
                        so = (ic % 2) * 512
                        it = v.tensor_mul(hT[:, ic * RH:(ic + 1) * RH],
                                          sil_t[:, so:so + RH], ps[:, pU:pU + RH])
                        sm_v.inc(it)
                    gu_ev.append(sm_v.v)
                hT_done = sm_v.v

                for icg in range(NICH // 2):
                    wd_box = []

                    @block.sync
                    def _(s):
                        DmaQ.wait(s, cast_done[e])
                        if len(st["wd_cons"]) >= 2:
                            s.wait_ge(sm_t.h, st["wd_cons"][-2])
                        wd_box.append(dqx.dma(
                            s, wd_sl[:, (icg % 2) * 2048:(icg % 2) * 2048 + 2048],
                            wd_bf[e][icg * 256:(icg + 1) * 256, :].rearrange(
                                "(ig p) d2 -> p ig d2", p=128)))

                    @block.tensor
                    def _(tn):
                        DmaQ.wait(tn, wd_box[0])
                        if icg == 0:
                            tn.wait_ge(sm_v.h, hT_done)
                            tn.wait_ge(sm_v.h, g_ready)
                            if st["down_ev"]:
                                tn.wait_ge(sm_v.h, st["down_ev"][-1])
                        for ig in range(2):
                            ic = icg * 2 + ig
                            wofs = (icg % 2) * 2048 + ig * 1024
                            for r2 in range(RH // 128):
                                for dh in range(2):
                                    it = tn.matmul(
                                        ps[:, r2 * 1024 + dh * 512:
                                           r2 * 1024 + (dh + 1) * 512],
                                        hT[:, ic * RH + r2 * 128:
                                           ic * RH + (r2 + 1) * 128],
                                        wd_sl[:, wofs + dh * 512:
                                              wofs + (dh + 1) * 512],
                                        start=(ic == 0), stop=(ic == NICH - 1))
                        sm_t.inc(it)
                    st["wd_cons"].append(sm_t.v)
                dm = sm_t.v

                @block.vector
                def _(v):
                    v.wait_ge(sm_t.h, dm)
                    for r2 in range(RH // 128):
                        rc = h * (RH // 128) + r2
                        for dh in range(2):
                            it = v.tensor_scalar(
                                y_pair[:, slot * RC * 1024 + rc * 1024 + dh * 512:
                                       slot * RC * 1024 + rc * 1024 + (dh + 1) * 512],
                                ps[:, r2 * 1024 + dh * 512:
                                   r2 * 1024 + (dh + 1) * 512],
                                g_cols[:, slot * RC + rc:slot * RC + rc + 1],
                                None, Alu.mult)
                    sm_v.inc(it)
                st["down_ev"].append(sm_v.v)
            return sm_v.v

        def emit_cbc(e, slot, cbc, y_ready, sel_done, after=0):
            marks = []
            for j in range(S // 512):
                @block.tensor
                def _(tn):
                    if j == 0:
                        tn.wait_ge(sm_v.h, max(y_ready, sel_done, after))
                    if marks:
                        tn.wait_ge(sm_v.h, marks[-1])
                    it = tn.matmul(ps[:, 3072:3072 + 512],
                                   e_sel[0:8, e * 128:(e + 1) * 128],
                                   ct16[0:8, j * 512:(j + 1) * 512],
                                   start=True, stop=True)
                    sm_t.inc(it)
                tm = sm_t.v

                @block.vector
                def _(v):
                    v.wait_ge(sm_t.h, tm)
                    it = v.tensor_copy(cbc[:, slot * S + j * 512:
                                           slot * S + (j + 1) * 512],
                                       ps[:, 3072:3072 + 512])
                    sm_v.inc(it)
                marks.append(sm_v.v)
            return sm_v.v

        def phase_scatter(call, src_dram, dst_dram, dst_is_state,
                          cbc, P_b, y_pair, cbc_ready, y_ready):
            pb_marks = []
            sc_ev = []
            store_marks = []
            entry_snap = dqx.snapshot()
            for c in range(NCH):
                buf = c % 2
                if c % 2 == 0:
                    @block.vector
                    def _(v):
                        v.wait_ge(sm_v.h, cbc_ready)
                        v.drain()
                        if pb_marks:
                            v.wait_ge(sm_t.h, pb_marks[-1][1])
                        for si in range(2):
                            for rc in range(RC):
                                it = v.tensor_scalar(
                                    P_b[:, (si * RC + rc) * 256:
                                        (si * RC + rc + 1) * 256],
                                    cbc[:, si * S + (c // 2) * 256:
                                        si * S + (c // 2 + 1) * 256],
                                    iota_pr[:, rc:rc + 1],
                                    None, Alu.is_equal)
                        sm_v.inc(it)
                    pb_marks.append([sm_v.v, 0])
                pbm = pb_marks[-1][0]

                xm_box = []

                @block.sync
                def _(s):
                    if c < 2:
                        DmaQ.wait(s, entry_snap)
                    if len(store_marks) >= 2:
                        DmaQ.wait(s, store_marks[-2])
                    if len(sc_ev) >= 2:
                        s.wait_ge(sm_v.h, sc_ev[-2])
                    if src_dram is x_state:
                        DmaQ.wait(s, xstate_mark[c])
                    xm_box.append(dqx.dma(s, xch[:, buf * 1024:(buf + 1) * 1024],
                                          src_dram[c * 128:(c + 1) * 128, :]))

                @block.tensor
                def _(tn):
                    tn.wait_ge(sm_v.h, pbm)
                    if c == 0:
                        tn.wait_ge(sm_v.h, y_ready)
                    if len(sc_ev) >= 2:
                        tn.wait_ge(sm_v.h, sc_ev[-2])
                    pofs = (c % 2) * 2048
                    for si in range(2):
                        for rc in range(RC):
                            for dh in range(2):
                                it = tn.matmul(
                                    ps[:, pofs + dh * 512:pofs + (dh + 1) * 512],
                                    P_b[:, (si * RC + rc) * 256 + (c % 2) * 128:
                                        (si * RC + rc) * 256 + (c % 2) * 128 + 128],
                                    y_pair[:, si * RC * 1024 + rc * 1024 + dh * 512:
                                           si * RC * 1024 + rc * 1024 + (dh + 1) * 512],
                                    start=(si == 0 and rc == 0),
                                    stop=(si == 1 and rc == RC - 1))
                    sm_t.inc(it)
                pb_marks[-1][1] = sm_t.v
                scm = sm_t.v

                @block.vector
                def _(v):
                    v.wait_ge(sm_t.h, scm)
                    DmaQ.wait(v, xm_box[0])
                    pofs = (c % 2) * 2048
                    it = v.tensor_add(xch[:, buf * 1024:buf * 1024 + 512],
                                      xch[:, buf * 1024:buf * 1024 + 512],
                                      ps[:, pofs:pofs + 512])
                    it = v.tensor_add(xch[:, buf * 1024 + 512:buf * 1024 + 1024],
                                      xch[:, buf * 1024 + 512:buf * 1024 + 1024],
                                      ps[:, pofs + 512:pofs + 1024])
                    sm_v.inc(it)
                sc_ev.append(sm_v.v)
                evm = sm_v.v

                @block.sync
                def _(s):
                    s.wait_ge(sm_v.h, evm)
                    mk = dqx.dma(s, dst_dram[c * 128:(c + 1) * 128, :],
                                 xch[:, buf * 1024:(buf + 1) * 1024])
                    store_marks.append(mk)
                    if dst_is_state:
                        xstate_mark[c] = mk

        def phase_attention(att):
            vst, wio_sb, xl2_ch, xc_s = (att["vst"], att["wio_sb"], att["xl2_ch"],
                                         att["xc_s"])
            probsT = att["probsT"]
            dist0a = xc_s[:, 0:384]
            prfa = xc_s[:, 384:768]

            tv0, vv0 = sm_t.v, sm_v.v

            @block.gpsimd
            def _(g):
                g.wait_ge(sm_t.h, tv0)
                g.wait_ge(sm_v.h, vv0)
                it = g.iota(dist0a, pattern=[[128, 3], [-1, 128]], base=-128,
                            channel_multiplier=1,
                            allow_small_or_imprecise_dtypes=True)
                sm_gc.inc(it)
            gcm = sm_gc.v

            for h in range(NH):
                std, off = ATTN_STD[h], ATTN_OFF[h]

                @block.vector
                def _(v, h=h, std=std, off=off):
                    if h == 0:
                        v.wait_ge(sm_gc.h, gcm)
                    v.drain()
                    it = v.tensor_scalar(prfa, dist0a, -off, 1.0 / std,
                                         Alu.add, Alu.mult)
                    v.drain()
                    it = v.tensor_mul(prfa, prfa, prfa)
                    sm_v.inc(it)
                vmk = sm_v.v

                @block.scalar
                def _(s, h=h):
                    s.wait_ge(sm_v.h, vmk)
                    it = s.activation(probsT[:, h * 384:(h + 1) * 384], prfa,
                                      Act.Exp, bias=0.0, scale=-0.5)
                    sm_s.inc(it)
                smk = sm_s.v

                @block.vector
                def _(v2, h=h, std=std):
                    v2.wait_ge(sm_s.h, smk)
                    inv_n = 1.0 / (std * math.sqrt(2.0 * math.pi))
                    it = v2.tensor_scalar(probsT[:, h * 384:(h + 1) * 384],
                                          probsT[:, h * 384:(h + 1) * 384],
                                          inv_n, None, Alu.mult)
                    sm_v.inc(it)
            pb_done = sm_v.v

            @block.vector
            def _(v):
                v.drain()

            wim_box = []

            @block.sync
            def _(s):
                DmaQ.wait(s, marks["wi_cast"])
                wim_box.append(dqx.dma(
                    s, wio_sb[:], wi_bf.rearrange("(dc p) o -> p dc o", p=128)))

            xl2_done = []
            vsl_ev = []
            vmm_m = []
            vps_ev = []
            ventry = dqx.snapshot()
            for c in range(NCH):
                buf = c % 2

                xm_box = []

                @block.sync
                def _(s):
                    if c < 2:
                        DmaQ.wait(s, ventry)
                    if c >= 2:
                        s.wait_ge(sm_v.h, xl2_done[c - 2])
                    DmaQ.wait(s, xstate_mark[c])
                    xm_box.append(dqx.dma(s, xch[:, buf * 1024:(buf + 1) * 1024],
                                          x_state[c * 128:(c + 1) * 128, :]))

                @block.vector
                def _(v):
                    DmaQ.wait(v, xm_box[0])
                xl2_done.append(
                    emit_ln_chunk(xch[:, buf * 1024:(buf + 1) * 1024],
                                  xl2_ch[:, buf * 1024:(buf + 1) * 1024]))
                lnm = xl2_done[-1]

                @block.tensor
                def _(tn):
                    tn.wait_ge(sm_v.h, lnm)
                    if vsl_ev:
                        tn.wait_ge(sm_v.h, vsl_ev[-1])
                    for dcc in range(DC):
                        it = tn.transpose(psb[:, dcc * 128:(dcc + 1) * 128],
                                          xl2_ch[:, buf * 1024 + dcc * 128:
                                              buf * 1024 + (dcc + 1) * 128],
                                          ident_b[:])
                    sm_t.inc(it)
                tm = sm_t.v

                @block.vector
                def _(v):
                    v.wait_ge(sm_t.h, tm)
                    if len(vmm_m) >= 2:
                        v.wait_ge(sm_t.h, vmm_m[-2])
                    it = v.tensor_copy(tsl[:, buf * 1024:(buf + 1) * 1024],
                                       psb[:, 0:1024])
                    sm_v.inc(it)
                vsl_ev.append(sm_v.v)
                evm = sm_v.v

                @block.tensor
                def _(tn):
                    tn.wait_ge(sm_v.h, evm)
                    DmaQ.wait(tn, wim_box[0])
                    if len(vps_ev) >= 2:
                        tn.wait_ge(sm_v.h, vps_ev[-2])
                    pofs = 1024 + (c % 2) * 1024
                    for nh2 in range(2):
                        for dcc in range(DC):
                            it = tn.matmul(
                                ps[:, pofs + nh2 * 512:pofs + (nh2 + 1) * 512],
                                tsl[:, buf * 1024 + dcc * 128:
                                    buf * 1024 + (dcc + 1) * 128],
                                wio_sb[:, dcc * 1024 + nh2 * 512:
                                       dcc * 1024 + (nh2 + 1) * 512],
                                start=(dcc == 0), stop=(dcc == DC - 1))
                    sm_t.inc(it)
                vmm_m.append(sm_t.v)
                vm = sm_t.v

                @block.vector
                def _(v):
                    v.wait_ge(sm_t.h, vm)
                    pofs = 1024 + (c % 2) * 1024
                    it = v.tensor_copy(vst[:, c * 1024:(c + 1) * 1024],
                                       ps[:, pofs:pofs + 1024])
                    sm_v.inc(it)
                vps_ev.append(sm_v.v)
            v_done = sm_v.v

            wom_box = []

            @block.sync
            def _(s):
                s.wait_ge(sm_t.h, vmm_m[-1])
                DmaQ.wait(s, marks["wo_cast"])
                wom_box.append(dqx.dma(
                    s, wio_sb[:], wo_bf.rearrange("(dc p) o -> p dc o", p=128)))

            att_ev = []
            att_tr = []
            atl_ev = []
            omm_m = []
            ops_ev = []
            oln_done = []
            ocp_ev = []
            store_marks = []
            oentry = dqx.snapshot()
            for c in range(NCH):
                buf = c % 2
                pofs = (c % 2) * 1024

                @block.tensor
                def _(tn):
                    tn.wait_ge(sm_v.h, v_done)
                    if len(att_ev) >= 2:
                        tn.wait_ge(sm_v.h, att_ev[-2])
                    js = [j for j in (-1, 0, 1) if 0 <= c + j < NCH]
                    for hh in range(NH):
                        for ji, j in enumerate(js):
                            it = tn.matmul(
                                ps[:, pofs + hh * 128:pofs + (hh + 1) * 128],
                                probsT[:, (hh * 3 + j + 1) * 128:
                                       (hh * 3 + j + 2) * 128],
                                vst[:, (c + j) * 1024 + hh * 128:
                                    (c + j) * 1024 + (hh + 1) * 128],
                                start=(ji == 0), stop=(ji == len(js) - 1))
                    sm_t.inc(it)
                am = sm_t.v

                @block.vector
                def _(v):
                    v.wait_ge(sm_t.h, am)
                    if len(att_tr) >= 2:
                        v.wait_ge(sm_t.h, att_tr[-2])
                    it = v.tensor_copy(att_bf[:, buf * 1024:(buf + 1) * 1024],
                                       ps[:, pofs:pofs + 1024])
                    sm_v.inc(it)
                att_ev.append(sm_v.v)
                aem = sm_v.v

                @block.tensor
                def _(tn):
                    tn.wait_ge(sm_v.h, aem)
                    if atl_ev:
                        tn.wait_ge(sm_v.h, atl_ev[-1])
                    bofs = 4096 + (c % 2) * 1024
                    for dcc in range(DC):
                        it = tn.transpose(psb[:, bofs + dcc * 128:
                                              bofs + (dcc + 1) * 128],
                                          att_bf[:, buf * 1024 + dcc * 128:
                                              buf * 1024 + (dcc + 1) * 128],
                                          ident_b[:])
                    sm_t.inc(it)
                att_tr.append(sm_t.v)
                atm = sm_t.v

                @block.vector
                def _(v):
                    v.wait_ge(sm_t.h, atm)
                    if len(omm_m) >= 2:
                        v.wait_ge(sm_t.h, omm_m[-2])
                    bofs = 4096 + (c % 2) * 1024
                    it = v.tensor_copy(tsl[:, buf * 1024:(buf + 1) * 1024],
                                       psb[:, bofs:bofs + 1024])
                    sm_v.inc(it)
                atl_ev.append(sm_v.v)
                aslm = sm_v.v

                xm_box = []

                @block.sync
                def _(s):
                    if c < 2:
                        DmaQ.wait(s, oentry)
                        s.wait_ge(sm_v.h, xl2_done[-1])
                    if len(store_marks) >= 2:
                        DmaQ.wait(s, store_marks[-2])
                    if len(ocp_ev) >= 2:
                        s.wait_ge(sm_v.h, ocp_ev[-2])
                    DmaQ.wait(s, xstate_mark[c])
                    xm_box.append(dqx.dma(s, xch[:, buf * 1024:(buf + 1) * 1024],
                                          x_state[c * 128:(c + 1) * 128, :]))

                @block.tensor
                def _(tn):
                    tn.wait_ge(sm_v.h, aslm)
                    DmaQ.wait(tn, wom_box[0])
                    if len(ops_ev) >= 2:
                        tn.wait_ge(sm_v.h, ops_ev[-2])
                    tofs = 2048 + (c % 2) * 1024
                    for nh2 in range(2):
                        for dcc in range(DC):
                            it = tn.matmul(
                                ps[:, tofs + nh2 * 512:tofs + (nh2 + 1) * 512],
                                tsl[:, buf * 1024 + dcc * 128:
                                    buf * 1024 + (dcc + 1) * 128],
                                wio_sb[:, dcc * 1024 + nh2 * 512:
                                       dcc * 1024 + (nh2 + 1) * 512],
                                start=(dcc == 0), stop=(dcc == DC - 1))
                    sm_t.inc(it)
                omm_m.append(sm_t.v)
                om = sm_t.v

                @block.vector
                def _(v):
                    v.wait_ge(sm_t.h, om)
                    DmaQ.wait(v, xm_box[0])
                    tofs = 2048 + (c % 2) * 1024
                    it = v.tensor_add(xch[:, buf * 1024:buf * 1024 + 512],
                                      xch[:, buf * 1024:buf * 1024 + 512],
                                      ps[:, tofs:tofs + 512])
                    it = v.tensor_add(xch[:, buf * 1024 + 512:buf * 1024 + 1024],
                                      xch[:, buf * 1024 + 512:buf * 1024 + 1024],
                                      ps[:, tofs + 512:tofs + 1024])
                    sm_v.inc(it)
                ops_ev.append(sm_v.v)
                adm = sm_v.v

                @block.sync
                def _(s):
                    s.wait_ge(sm_v.h, adm)
                    a_dst = out if cfg.stop_after == "attn" else x_state
                    mk = dqx.dma(s, a_dst[c * 128:(c + 1) * 128, :],
                                 xch[:, buf * 1024:(buf + 1) * 1024])
                    store_marks.append(mk)
                    xstate_mark[c] = mk

                @block.vector
                def _(v):
                    v.drain()
                    it = v.tensor_copy(xc_s[:], xch[:, buf * 1024:(buf + 1) * 1024])
                    sm_v.inc(it)
                ocp_ev.append(sm_v.v)
                oln_done.append(emit_ln_chunk(xc_s[:], xln[:, c * 1024:(c + 1) * 1024]))

        for call in (1, 2):
            sel_es = ExitStack()
            with sel_es:
                aff_T = sel_es.enter_context(
                    nc.sbuf_tensor(f"aff_T{call}", [8, S], F32))
                mask_T = sel_es.enter_context(
                    nc.sbuf_tensor(f"mask_T{call}", [8, S], F32))
                ct32 = sel_es.enter_context(
                    nc.sbuf_tensor(f"ct32_{call}", [8, S], F32))
                bis_junk = sel_es.enter_context(
                    nc.sbuf_tensor(f"bj{call}", [8, S], FP16))
                lo_t = sel_es.enter_context(nc.sbuf_tensor(f"lo{call}", [8, 1], F32))
                gi_t = sel_es.enter_context(nc.sbuf_tensor(f"gi{call}", [8, 1], U8))
                hi_t = sel_es.enter_context(nc.sbuf_tensor(f"hi{call}", [8, 1], F32))
                mid_t = sel_es.enter_context(nc.sbuf_tensor(f"mid{call}", [8, 1], F32))
                cnt_t = sel_es.enter_context(nc.sbuf_tensor(f"cnt{call}", [8, 1], F32))
                ge_t = sel_es.enter_context(nc.sbuf_tensor(f"ge{call}", [8, 1], U8))

                barrier()
                phase_ln_router(call, x_in if call == 1 else None, aff_T)
                barrier()
                sel_done = phase_selection(
                    call, (aff_T, mask_T, ct32, bis_junk, lo_t, hi_t, mid_t,
                           cnt_t, ge_t, gi_t))

            exp_es = ExitStack()
            with exp_es:
                x_inT = exp_es.enter_context(
                    nc.sbuf_tensor(f"x_inT{call}", [128, DC * K], BF16))
                y_pair = exp_es.enter_context(
                    nc.sbuf_tensor(f"y_pair{call}", [128, 2 * RC * 1024], BF16))
                PT_b = exp_es.enter_context(
                    nc.sbuf_tensor(f"PT_b{call}", [128, 2 * 512], BF16))

                st = dict(x_inT=x_inT, y_pair=y_pair, PT_b=PT_b,
                          pt_cons=[], gath_ev=[], down_ev=[],
                          wgu_dma=[], wgu_marks=[], wd_cons=[])
                for pair in range(E // 2):
                    e1, e2 = 2 * pair, 2 * pair + 1
                    barrier()
                    work_es = ExitStack()
                    with work_es:
                        st["hT"] = work_es.enter_context(
                            nc.sbuf_tensor(f"hT{call}_{pair}", [128, NICH * RH],
                                           BF16))
                        st["wgu_sl"] = work_es.enter_context(
                            nc.sbuf_tensor(f"wgu{call}_{pair}", [128, 2 * 4096],
                                           BF16))
                        st["wd_sl"] = work_es.enter_context(
                            nc.sbuf_tensor(f"wd{call}_{pair}", [128, 2 * 2048],
                                           BF16))
                        phase_expert(call, e1, 0, st, sel_done)
                        phase_expert(call, e2, 1, st, sel_done)
                        y_ready = sm_v.v
                    barrier()
                    sc_es = ExitStack()
                    with sc_es:
                        cbc = sc_es.enter_context(
                            nc.sbuf_tensor(f"cbc{call}_{pair}", [128, 2 * S], FP16))
                        P_b = sc_es.enter_context(
                            nc.sbuf_tensor(f"P_b{call}_{pair}", [128, 16 * 256],
                                           BF16))
                        cm1 = emit_cbc(e1, 0, cbc, y_ready, sel_done)
                        cbc_ready = emit_cbc(e2, 1, cbc, y_ready, sel_done,
                                             after=cm1)
                        psrc = (x_in if (call == 1 and pair == 0) else x_state)
                        last = (pair == E // 2 - 1)
                        stop1 = (cfg.stop_after == "ffn1" and call == 1)
                        pdst = out if (last and (call == 2 or stop1)) else x_state
                        phase_scatter(call, psrc, pdst, not (call == 2 and last),
                                      cbc, P_b, y_pair, cbc_ready, y_ready)

            if cfg.stop_after == "ffn1":
                break
            if call == 1:
                barrier()
                att_es = ExitStack()
                with att_es:
                    att = dict(
                        vst=att_es.enter_context(
                            nc.sbuf_tensor("vst", [128, NCH * 1024], BF16)),
                        wio_sb=att_es.enter_context(
                            nc.sbuf_tensor("wio_sb", [128, DC * 1024], BF16)),
                        xl2_ch=att_es.enter_context(
                            nc.sbuf_tensor("xl2_ch", [128, 2 * 1024], BF16)),
                        xc_s=att_es.enter_context(
                            nc.sbuf_tensor("xc_s", [128, 1024], F32)),
                        probsT=att_es.enter_context(
                            nc.sbuf_tensor("probsT", [128, NH * 3 * 128], BF16)))
                    att_bf = att_es.enter_context(
                        nc.sbuf_tensor("att_bf", [128, 2 * 1024], BF16))
                    phase_attention(att)
                barrier()
            if cfg.stop_after == "attn":
                break

        @block.sync
        def _(s):
            dqx.wait_all(s)
            dqg.wait_all(s)

    return nc


def kernel(L0_attn_values, L0_attn_input_weights, L0_attn_output_weight,
           L0_ffn_gate_w, L0_ffn_w_gate, L0_ffn_w_up, L0_ffn_w_down):
    cfg = Cfg()
    B = L0_attn_values.shape[0]
    x = np.ascontiguousarray(np.asarray(L0_attn_values, dtype=np.float32))
    wgT = np.ascontiguousarray(np.asarray(L0_ffn_w_gate, np.float32).transpose(0, 2, 1))
    wuT = np.ascontiguousarray(np.asarray(L0_ffn_w_up, np.float32).transpose(0, 2, 1))
    wdT = np.ascontiguousarray(np.asarray(L0_ffn_w_down, np.float32).transpose(0, 2, 1))
    gwT = np.ascontiguousarray(np.asarray(L0_ffn_gate_w, np.float32).T)
    wiT = np.ascontiguousarray(np.asarray(L0_attn_input_weights, np.float32).T)
    woT = np.ascontiguousarray(np.asarray(L0_attn_output_weight, np.float32).T)

    nc = build_program(cfg)
    in_maps = []
    for b in range(B):
        in_maps.append(dict(x_in=np.ascontiguousarray(x[b]), wgT=wgT, wuT=wuT,
                            wdT=wdT, gwT=gwT, wiT=wiT, woT=woT))
    res = run_bass_kernel_spmd(nc, in_maps, core_ids=list(range(B)))
    return np.stack([res.results[b]["out"] for b in range(B)], axis=0)


# version 11
# speedup vs baseline: 1.0387x; 1.0387x over previous
import math
from contextlib import ExitStack

import numpy as np
import ml_dtypes

import concourse.bass as bass
import concourse.mybir as mybir
from concourse.bass_utils import run_bass_kernel_spmd

F32 = mybir.dt.float32
U8 = mybir.dt.uint8
BF16 = mybir.dt.bfloat16
FP16 = mybir.dt.float16
Act = mybir.ActivationFunctionType
Alu = mybir.AluOpType
AX = mybir.AxisListType.X

ATTN_STD = [1.0, 2.0, 4.0, 8.0, 1.0, 2.0, 4.0, 8.0]
ATTN_OFF = [-1.0, -2.0, -4.0, -8.0, -1.0, -2.0, -4.0, -8.0]


class Cfg:
    def __init__(self, nch=32, nich=32, bisect=36, stop_after=None):
        self.stop_after = stop_after
        self.D = 1024
        self.DC = 8
        self.NCH = nch
        self.S = 128 * nch
        self.NICH = nich
        self.I = 128 * nich
        self.E = 8
        self.K = self.S // self.E * 2
        self.RC = self.K // 128
        self.RH = self.K // 2
        self.NH = 8
        self.BISECT = bisect


class Sem:
    def __init__(self, es, nc, name):
        self.h = es.enter_context(nc.semaphore(name))
        self.v = 0

    def inc(self, instr, n=1):
        instr.then_inc(self.h, n)
        self.v += n
        return self.v


class DmaQ:
    def __init__(self, es, nc, name, n):
        self.sems = [Sem(es, nc, f"{name}{i}") for i in range(n)]
        self.i = 0

    def dma(self, eng, out, in_):
        s = self.sems[self.i % len(self.sems)]
        eng.wait_ge(s.h, s.v)
        it = eng.dma_start(out=out, in_=in_)
        s.inc(it, 16)
        self.i += 1
        return (s, s.v)

    @staticmethod
    def wait(eng, mark):
        if mark is None:
            return
        if isinstance(mark, list):
            for m in mark:
                eng.wait_ge(m[0].h, m[1])
        else:
            eng.wait_ge(mark[0].h, mark[1])

    def snapshot(self):
        return [(s, s.v) for s in self.sems]

    def wait_all(self, eng):
        for s in self.sems:
            eng.wait_ge(s.h, s.v)


def build_program(cfg: Cfg):
    nc = bass.Bass("TRN2", target_bir_lowering=False)
    D, DC, NCH, E, K, RC, RH = cfg.D, cfg.DC, cfg.NCH, cfg.E, cfg.K, cfg.RC, cfg.RH
    S, I, NICH, NH = cfg.S, cfg.I, cfg.NICH, cfg.NH

    x_in = nc.dram_tensor("x_in", [S, D], F32, kind="ExternalInput").ap()
    wgT = nc.dram_tensor("wgT", [E, D, I], F32, kind="ExternalInput").ap()
    wuT = nc.dram_tensor("wuT", [E, D, I], F32, kind="ExternalInput").ap()
    wdT = nc.dram_tensor("wdT", [E, I, D], F32, kind="ExternalInput").ap()
    gwT = nc.dram_tensor("gwT", [D, E], F32, kind="ExternalInput").ap()
    wiT = nc.dram_tensor("wiT", [D, D], F32, kind="ExternalInput").ap()
    woT = nc.dram_tensor("woT", [D, D], F32, kind="ExternalInput").ap()
    out = nc.dram_tensor("out", [S, D], F32, kind="ExternalOutput").ap()
    x_state = nc.dram_tensor("x_state", [S, D], F32, kind="Internal").ap()
    wg_bf = nc.dram_tensor("wg_bf", [E, D, I], BF16, kind="Internal").ap()
    wu_bf = nc.dram_tensor("wu_bf", [E, D, I], BF16, kind="Internal").ap()
    wd_bf = nc.dram_tensor("wd_bf", [E, I, D], BF16, kind="Internal").ap()
    wi_bf = nc.dram_tensor("wi_bf", [D, D], BF16, kind="Internal").ap()
    wo_bf = nc.dram_tensor("wo_bf", [D, D], BF16, kind="Internal").ap()
    gw_bf = nc.dram_tensor("gw_bf", [D, E], BF16, kind="Internal").ap()

    es = ExitStack()
    sb = lambda name, shape, dt: es.enter_context(nc.sbuf_tensor(name, shape, dt))

    with es:
        xln = sb("xln", [128, NCH * 1024], BF16)
        tsl = sb("tsl", [128, 2 * 1024], BF16)
        xch = sb("xch", [128, 2 * 1024], F32)
        a_cols = sb("a_cols", [128, NCH * E], F32)
        a_cols_bf = sb("a_cols_bf", [128, NCH * E], BF16)
        ct16 = sb("ct16", [8, S], FP16)
        ct_cols = sb("ct_cols", [128, NCH * E], F32)
        iota_r = sb("iota_r", [128, 1024], FP16)
        iota_pr = sb("iota_pr", [128, RC], F32)
        ident_b = sb("ident_b", [128, 128], BF16)
        ident_f = sb("ident_f", [128, 128], F32)
        ones16 = sb("ones16", [1, 128], FP16)
        e_sel = sb("e_sel", [8, 8 * 128], FP16)
        gw_sb = sb("gw_sb", [128, DC * E], BF16)
        g_cols = sb("g_cols", [128, 2 * RC], F32)
        g_row = sb("g_row", [1, K], F32)
        mneg = sb("mneg", [128, 1], F32)
        lnt = sb("lnt", [128, 1], F32)
        vsum = sb("vsum", [128, 1], F32)
        rstd = sb("rstd", [128, 1], F32)
        negmx = sb("negmx", [128, 1], F32)
        ssum = sb("ssum", [128, 1], F32)
        srcp = sb("srcp", [128, 1], F32)
        aexp = sb("aexp", [128, E], F32)
        sil_t = sb("sil_t", [128, 2 * 512], F32)
        sq_junk = sil_t[:].bitcast(BF16)[:, 0:1024]
        const_sb = sb("const_sb", [128, 2], F32)
        nc.const_aps.register(F32, 0.0, const_sb[:, 0:1])
        nc.const_aps.register(F32, 1e-5, const_sb[:, 1:2])

        dqx = DmaQ(es, nc, "dqx", 8)
        dqg = DmaQ(es, nc, "dqg", 2)
        sm_gc = Sem(es, nc, "sm_gc")
        sm_t = Sem(es, nc, "sm_t")
        sm_v = Sem(es, nc, "sm_v")
        sm_s = Sem(es, nc, "sm_s")
        block = es.enter_context(nc.Block())
        ps = es.enter_context(nc.psum_tensor("ps", [128, 4096], F32))
        psb = ps[:].bitcast(BF16)

        xstate_mark = [None] * NCH
        marks = {}
        dist0 = xch[:, 0:384]
        prf = xch[:, 384:768]
        e_sel_i = xch[0:8, 768:1792]

        def barrier():
            tv, vv, sv = sm_t.v, sm_v.v, sm_s.v

            @block.tensor
            def _(tn):
                tn.wait_ge(sm_v.h, vv)
                tn.wait_ge(sm_s.h, sv)

            @block.vector
            def _(v):
                v.wait_ge(sm_t.h, tv)
                v.wait_ge(sm_s.h, sv)
                v.drain()

            @block.scalar
            def _(s):
                s.wait_ge(sm_t.h, tv)
                s.wait_ge(sm_v.h, vv)
                s.drain()

        @block.gpsimd
        def _(g):
            it = g.iota(iota_r[:], pattern=[[1, 1024]], base=0, channel_multiplier=0,
                        allow_small_or_imprecise_dtypes=True)
            it = g.iota(iota_pr[:], pattern=[[128, RC]], base=0, channel_multiplier=1,
                        allow_small_or_imprecise_dtypes=True)
            it = g.iota(prf[:, 0:128], pattern=[[1, 128]], base=0,
                        channel_multiplier=-1, allow_small_or_imprecise_dtypes=True)
            it = g.iota(e_sel_i, pattern=[[-1, 8], [0, 128]], base=0,
                        channel_multiplier=1, allow_small_or_imprecise_dtypes=True)
            it = g.memset(ones16[:], 1.0)
            sm_gc.inc(it, 5)
            marks["gw_cast"] = dqg.dma(g, gw_bf, gwT)
            marks["wi_cast"] = dqg.dma(g, wi_bf, wiT)
            marks["wo_cast"] = dqg.dma(g, wo_bf, woT)

        cast_done = {}

        @block.gpsimd
        def _(g):
            for e in range(E):
                dqg.dma(g, wg_bf[e], wgT[e])
                dqg.dma(g, wu_bf[e], wuT[e])
                dqg.dma(g, wd_bf[e], wdT[e])
                cast_done[e] = dqg.snapshot()

        @block.vector
        def _(v):
            v.memset(const_sb[:, 0:1], 0.0)
            v.memset(const_sb[:, 1:2], 1e-5)
            v.drain()
            v.wait_ge(sm_gc.h, 5)
            it = v.tensor_scalar(e_sel[:], e_sel_i, 0.0, None, Alu.is_equal)
            it = v.tensor_scalar(ident_b[:], prf[:, 0:128], 0.0, None, Alu.is_equal)
            it = v.tensor_scalar(ident_f[:], prf[:, 0:128], 0.0, None, Alu.is_equal)
            sm_v.inc(it)

        marks["const_vm"] = sm_v.v

        @block.sync
        def _(s):
            DmaQ.wait(s, marks["gw_cast"])
            marks["gw_sb"] = dqx.dma(s, gw_sb[:],
                                     gw_bf.rearrange("(c p) e -> p c e", p=128))

        def emit_ln_chunk(src_ap, dst_ap):
            @block.vector
            def _(v):
                v.drain()
                it = v.tensor_reduce(mneg[:], src_ap, AX, Alu.add, negate=True)
                v.drain()
                it = v.tensor_scalar(lnt[:], mneg[:], 1.0 / D, None, Alu.mult)
                v.drain()
                it = v.tensor_scalar(src_ap, src_ap, lnt[:, 0:1], None, Alu.add)
                v.drain()
                it = v.scalar_tensor_tensor(sq_junk, src_ap, 1.0, src_ap,
                                            Alu.mult, Alu.mult, accum_out=vsum[:])
                sm_v.inc(it)
            m1 = sm_v.v

            @block.scalar
            def _(s):
                s.wait_ge(sm_v.h, m1)
                it = s.activation(rstd[:], vsum[:], Act.Ln, bias=1e-5, scale=1.0 / D)
                s.drain()
                it = s.activation(rstd[:], rstd[:], Act.Exp, bias=0.0, scale=-0.5)
                sm_s.inc(it)
            m2 = sm_s.v

            @block.vector
            def _(v):
                v.wait_ge(sm_s.h, m2)
                it = v.tensor_scalar(dst_ap, src_ap, rstd[:, 0:1], None, Alu.mult)
                sm_v.inc(it)
            return sm_v.v

        def phase_ln_router(call, src_dram, aff_T):
            slab_ev = []
            router_m = []
            smx_cons = []
            ln_done = []
            aT_ev = []
            for c in range(NCH):
                buf = c % 2
                if src_dram is not None:
                    xm_box = []

                    @block.sync
                    def _(s):
                        if c < 2:
                            s.wait_ge(sm_v.h, marks["const_vm"])
                        if c >= 2:
                            s.wait_ge(sm_v.h, ln_done[c - 2])
                        xm_box.append(
                            dqx.dma(s, xch[:, buf * 1024:(buf + 1) * 1024],
                                    src_dram[c * 128:(c + 1) * 128, :]))

                    @block.vector
                    def _(v):
                        DmaQ.wait(v, xm_box[0])
                    ln_done.append(
                        emit_ln_chunk(xch[:, buf * 1024:(buf + 1) * 1024],
                                      xln[:, c * 1024:(c + 1) * 1024]))
                lnm = sm_v.v

                @block.tensor
                def _(tn):
                    tn.wait_ge(sm_v.h, lnm)
                    if slab_ev:
                        tn.wait_ge(sm_v.h, slab_ev[-1])
                    for dcc in range(DC):
                        it = tn.transpose(psb[:, dcc * 128:(dcc + 1) * 128],
                                          xln[:, c * 1024 + dcc * 128:
                                              c * 1024 + (dcc + 1) * 128],
                                          ident_b[:])
                    sm_t.inc(it)
                ttm = sm_t.v

                @block.vector
                def _(v):
                    v.wait_ge(sm_t.h, ttm)
                    if len(router_m) >= 2:
                        v.wait_ge(sm_t.h, router_m[-2])
                    it = v.tensor_copy(tsl[:, buf * 1024:(buf + 1) * 1024],
                                       psb[:, 0:1024])
                    sm_v.inc(it)
                slab_ev.append(sm_v.v)
                evm = sm_v.v

                @block.tensor
                def _(tn):
                    tn.wait_ge(sm_v.h, evm)
                    DmaQ.wait(tn, marks["gw_sb"])
                    if len(smx_cons) >= 2:
                        tn.wait_ge(sm_s.h, smx_cons[-2][0])
                        tn.wait_ge(sm_v.h, smx_cons[-2][1])
                    rbase = 1024 + (c % 2) * 512
                    for dcc in range(DC):
                        it = tn.matmul(ps[:, rbase:rbase + E],
                                       tsl[:, buf * 1024 + dcc * 128:
                                           buf * 1024 + (dcc + 1) * 128],
                                       gw_sb[:, dcc * E:(dcc + 1) * E],
                                       start=(dcc == 0), stop=(dcc == DC - 1))
                    sm_t.inc(it)
                router_m.append(sm_t.v)
                rm = sm_t.v

                @block.vector
                def _(v):
                    v.wait_ge(sm_t.h, rm)
                    rbase = 1024 + (c % 2) * 512
                    it = v.reduce_max(negmx[:], ps[:, rbase:rbase + E], AX,
                                      negate=True)
                    sm_v.inc(it)
                nm = sm_v.v

                @block.scalar
                def _(s):
                    s.wait_ge(sm_v.h, nm)
                    rbase = 1024 + (c % 2) * 512
                    it = s.activation(aexp[:], ps[:, rbase:rbase + E], Act.Exp,
                                      bias=negmx[:, 0:1], scale=1.0)
                    sm_s.inc(it)
                sx = sm_s.v

                @block.vector
                def _(v):
                    v.wait_ge(sm_s.h, sx)
                    it = v.tensor_reduce(ssum[:], aexp[:], AX, Alu.add)
                    v.drain()
                    it = v.reciprocal(srcp[:], ssum[:])
                    v.drain()
                    it = v.tensor_scalar(a_cols[:, c * E:(c + 1) * E], aexp[:],
                                         srcp[:, 0:1], 0.5, Alu.mult, Alu.mult)
                    sm_v.inc(it)
                smx_cons.append((sx, sm_v.v))
                am = sm_v.v

                @block.tensor
                def _(tn):
                    tn.wait_ge(sm_v.h, am)
                    if aT_ev:
                        tn.wait_ge(sm_v.h, aT_ev[-1])
                    it = tn.transpose(ps[0:E, 2048:2048 + 128],
                                      a_cols[:, c * E:(c + 1) * E], ident_f[:])
                    sm_t.inc(it)
                t2 = sm_t.v

                @block.vector
                def _(v):
                    v.wait_ge(sm_t.h, t2)
                    it = v.tensor_copy(aff_T[:, c * 128:(c + 1) * 128],
                                       ps[0:E, 2048:2048 + 128])
                    sm_v.inc(it)
                aT_ev.append(sm_v.v)

        def phase_selection(call, sel):
            aff_T, mask_T, ct32, bis_junk, lo_t, hi_t, mid_t, cnt_t, ge_t, gi_t = sel

            @block.vector
            def _(v):
                v.drain()
                it = v.tensor_copy(a_cols_bf[:], a_cols[:])
                v.drain()
                it = v.memset(lo_t[:], 0.0)
                it = v.memset(hi_t[:], 1.0)
                it = v.memset(mid_t[:], 0.5)
                v.drain()
                for _i in range(cfg.BISECT):
                    it = v.tensor_scalar(bis_junk[:], aff_T[:], mid_t[:, 0:1], None,
                                         Alu.is_ge, Alu.add, accum_out=cnt_t[:])
                    v.drain()
                    it = v.tensor_scalar(ge_t[:], cnt_t[:], float(K) - 0.5, None,
                                         Alu.is_ge)
                    it = v.tensor_scalar(gi_t[:], cnt_t[:], float(K) - 0.5, None,
                                         Alu.is_lt)
                    v.drain()
                    it = v.select(lo_t[:], ge_t[:], mid_t[:], lo_t[:],
                                  add_drain=True)
                    v.drain()
                    it = v.select(hi_t[:], gi_t[:], mid_t[:], hi_t[:],
                                  add_drain=True)
                    v.drain()
                    it = v.tensor_scalar(mid_t[:], lo_t[:], hi_t[:, 0:1], 0.5,
                                         Alu.add, Alu.mult)
                    v.drain()
                it = v.tensor_scalar(mask_T[:], aff_T[:], lo_t[:, 0:1], None,
                                     Alu.is_ge)
                v.drain()
                it = v.tensor_tensor_scan(ct32[:], mask_T[:], mask_T[:], 0.0,
                                          Alu.add, Alu.bypass)
                v.drain()
                it = v.tensor_mul(ct32[:], ct32[:], mask_T[:])
                v.drain()
                it = v.tensor_scalar(ct32[:], ct32[:], 1.0, None, Alu.subtract)
                v.drain()
                it = v.tensor_copy(ct16[:], ct32[:])
                sm_v.inc(it)
            selm = sm_v.v

            prev = []
            for c in range(NCH):
                @block.tensor
                def _(tn):
                    tn.wait_ge(sm_v.h, selm)
                    if len(prev) >= 2:
                        tn.wait_ge(sm_v.h, prev[-2])
                    pofs = 2048 + (c % 2) * 512
                    it = tn.transpose(ps[0:128, pofs:pofs + E],
                                      ct32[:, c * 128:(c + 1) * 128],
                                      ident_f[0:E, 0:E])
                    sm_t.inc(it)
                tm = sm_t.v

                @block.vector
                def _(v):
                    v.wait_ge(sm_t.h, tm)
                    pofs = 2048 + (c % 2) * 512
                    it = v.tensor_copy(ct_cols[:, c * E:(c + 1) * E],
                                       ps[0:128, pofs:pofs + E])
                    sm_v.inc(it)
                prev.append(sm_v.v)
            return sm_v.v

        def phase_expert(call, e, slot, st, sel_done):
            x_inT, y_pair, PT_b, hT, wgu_sl, wd_sl = (
                st["x_inT"], st["y_pair"], st["PT_b"], st["hT"], st["wgu_sl"],
                st["wd_sl"])
            pt_cons = st["pt_cons"]

            for h in range(2):
                for grp in range(2):
                    for c in range(NCH):
                        rslot = c % 4
                        if rslot < 2:
                            pt_ap = PT_b[:, rslot * RH:(rslot + 1) * RH]
                        else:
                            pt_ap = tsl[:, (rslot - 2) * RH:(rslot - 1) * RH]

                        @block.vector
                        def _(v):
                            if c == 0 and h == 0 and grp == 0:
                                v.wait_ge(sm_v.h, sel_done)
                                v.drain()
                            if len(pt_cons) >= 4:
                                v.wait_ge(sm_t.h, pt_cons[-4])
                            it = v.tensor_scalar(
                                pt_ap,
                                iota_r[:, h * RH:h * RH + RH],
                                ct_cols[:, c * E + e:c * E + e + 1],
                                None, Alu.is_equal)
                            sm_v.inc(it)
                        vm = sm_v.v

                        @block.tensor
                        def _(tn):
                            tn.wait_ge(sm_v.h, vm)
                            if c == 0:
                                if st["gath_ev"]:
                                    tn.wait_ge(sm_v.h, st["gath_ev"][-1])
                                if st["down_ev"]:
                                    tn.wait_ge(sm_v.h, st["down_ev"][-1])
                            for dd in range(4):
                                dcc = grp * 4 + dd
                                it = tn.matmul(
                                    ps[:, dd * 512:dd * 512 + RH],
                                    xln[:, c * 1024 + dcc * 128:
                                        c * 1024 + (dcc + 1) * 128],
                                    pt_ap,
                                    start=(c == 0), stop=(c == NCH - 1))
                            if grp == 0:
                                it = tn.matmul(
                                    ps[0:1, 2048:2048 + RH],
                                    a_cols_bf[:, c * E + e:c * E + e + 1],
                                    pt_ap,
                                    start=(c == 0), stop=(c == NCH - 1))
                            sm_t.inc(it)
                        pt_cons.append(sm_t.v)
                    gm = sm_t.v

                    @block.vector
                    def _(v):
                        v.wait_ge(sm_t.h, gm)
                        for dd in range(4):
                            dcc = grp * 4 + dd
                            it = v.tensor_copy(
                                x_inT[:, dcc * K + h * RH:dcc * K + h * RH + RH],
                                ps[:, dd * 512:dd * 512 + RH])
                        if grp == 0:
                            it = v.tensor_copy(g_row[0:1, h * RH:h * RH + RH],
                                               ps[0:1, 2048:2048 + RH])
                        sm_v.inc(it)
                    st["gath_ev"].append(sm_v.v)
            xin_ready = sm_v.v

            prev = []
            for j in range(RC):
                @block.tensor
                def _(tn):
                    tn.wait_ge(sm_v.h, xin_ready)
                    if len(prev) >= 2:
                        tn.wait_ge(sm_v.h, prev[-2])
                    pofs = 2560 + (j % 2)
                    it = tn.transpose(ps[:, pofs:pofs + 1],
                                      g_row[0:1, j * 128:(j + 1) * 128],
                                      ident_f[0:1, 0:1])
                    sm_t.inc(it)
                tj = sm_t.v

                @block.vector
                def _(v):
                    v.wait_ge(sm_t.h, tj)
                    pofs = 2560 + (j % 2)
                    it = v.tensor_copy(g_cols[:, slot * RC + j:slot * RC + j + 1],
                                       ps[:, pofs:pofs + 1])
                    sm_v.inc(it)
                prev.append(sm_v.v)
            g_ready = sm_v.v

            for h in range(2):
                gu_ev = []
                for ic in range(NICH):
                    if ic % 2 == 0:
                        @block.sync
                        def _(s):
                            DmaQ.wait(s, cast_done[e])
                            if len(st["wgu_marks"]) >= 3:
                                s.wait_ge(sm_t.h, st["wgu_marks"][-3])
                            i0 = ic * 128
                            bofs = (ic // 2 % 2) * 2048
                            m1_ = dqx.dma(s, wgu_sl[:, bofs:bofs + 2048],
                                          wg_bf[e][:, i0:i0 + 256].rearrange(
                                              "(dc p) i -> p dc i", p=128))
                            m2_ = dqx.dma(
                                s, wgu_sl[:, 4096 + bofs:4096 + bofs + 2048],
                                wu_bf[e][:, i0:i0 + 256].rearrange(
                                    "(dc p) i -> p dc i", p=128))
                            st["wgu_dma"].append([m1_, m2_])
                    wm = st["wgu_dma"][-1]

                    @block.tensor
                    def _(tn):
                        DmaQ.wait(tn, wm)
                        if ic == 0:
                            tn.wait_ge(sm_v.h, xin_ready)
                            if st["down_ev"]:
                                tn.wait_ge(sm_v.h, st["down_ev"][-1])
                        if len(gu_ev) >= 2:
                            tn.wait_ge(sm_v.h, gu_ev[-2])
                        base = (ic // 2 % 2) * 2048 + (ic % 2) * 128
                        pG = (ic % 2) * 512
                        pU = 1024 + (ic % 2) * 512
                        for dcc in range(DC):
                            it = tn.matmul(
                                ps[:, pG:pG + RH],
                                wgu_sl[:, base + dcc * 256:base + dcc * 256 + 128],
                                x_inT[:, dcc * K + h * RH:dcc * K + h * RH + RH],
                                start=(dcc == 0), stop=(dcc == DC - 1))
                            it = tn.matmul(
                                ps[:, pU:pU + RH],
                                wgu_sl[:, 4096 + base + dcc * 256:
                                       4096 + base + dcc * 256 + 128],
                                x_inT[:, dcc * K + h * RH:dcc * K + h * RH + RH],
                                start=(dcc == 0), stop=(dcc == DC - 1))
                        sm_t.inc(it)
                    st["wgu_marks"].append(sm_t.v)
                    gum = sm_t.v

                    @block.scalar
                    def _(s):
                        s.wait_ge(sm_t.h, gum)
                        if len(gu_ev) >= 2:
                            s.wait_ge(sm_v.h, gu_ev[-2])
                        pG = (ic % 2) * 512
                        so = (ic % 2) * 512
                        it = s.activation(sil_t[:, so:so + RH], ps[:, pG:pG + RH],
                                          Act.Silu, bias=0.0, scale=1.0)
                        sm_s.inc(it)
                    sm2 = sm_s.v

                    @block.vector
                    def _(v):
                        v.wait_ge(sm_s.h, sm2)
                        pU = 1024 + (ic % 2) * 512
                        so = (ic % 2) * 512
                        it = v.tensor_mul(hT[:, ic * RH:(ic + 1) * RH],
                                          sil_t[:, so:so + RH], ps[:, pU:pU + RH])
                        sm_v.inc(it)
                    gu_ev.append(sm_v.v)
                hT_done = sm_v.v

                for icg in range(NICH // 2):
                    wd_box = []

                    @block.sync
                    def _(s):
                        DmaQ.wait(s, cast_done[e])
                        if len(st["wd_cons"]) >= 2:
                            s.wait_ge(sm_t.h, st["wd_cons"][-2])
                        wd_box.append(dqx.dma(
                            s, wd_sl[:, (icg % 2) * 2048:(icg % 2) * 2048 + 2048],
                            wd_bf[e][icg * 256:(icg + 1) * 256, :].rearrange(
                                "(ig p) d2 -> p ig d2", p=128)))

                    @block.tensor
                    def _(tn):
                        DmaQ.wait(tn, wd_box[0])
                        if icg == 0:
                            tn.wait_ge(sm_v.h, hT_done)
                            tn.wait_ge(sm_v.h, g_ready)
                            if st["down_ev"]:
                                tn.wait_ge(sm_v.h, st["down_ev"][-1])
                        for ig in range(2):
                            ic = icg * 2 + ig
                            wofs = (icg % 2) * 2048 + ig * 1024
                            for r2 in range(RH // 128):
                                for dh in range(2):
                                    it = tn.matmul(
                                        ps[:, r2 * 1024 + dh * 512:
                                           r2 * 1024 + (dh + 1) * 512],
                                        hT[:, ic * RH + r2 * 128:
                                           ic * RH + (r2 + 1) * 128],
                                        wd_sl[:, wofs + dh * 512:
                                              wofs + (dh + 1) * 512],
                                        start=(ic == 0), stop=(ic == NICH - 1))
                        sm_t.inc(it)
                    st["wd_cons"].append(sm_t.v)
                dm = sm_t.v

                @block.vector
                def _(v):
                    v.wait_ge(sm_t.h, dm)
                    for r2 in range(RH // 128):
                        rc = h * (RH // 128) + r2
                        for dh in range(2):
                            it = v.tensor_scalar(
                                y_pair[:, slot * RC * 1024 + rc * 1024 + dh * 512:
                                       slot * RC * 1024 + rc * 1024 + (dh + 1) * 512],
                                ps[:, r2 * 1024 + dh * 512:
                                   r2 * 1024 + (dh + 1) * 512],
                                g_cols[:, slot * RC + rc:slot * RC + rc + 1],
                                None, Alu.mult)
                    sm_v.inc(it)
                st["down_ev"].append(sm_v.v)
            return sm_v.v

        def emit_cbc(e, slot, cbc, y_ready, sel_done, after=0):
            marks = []
            for j in range(S // 512):
                @block.tensor
                def _(tn):
                    if j == 0:
                        tn.wait_ge(sm_v.h, max(y_ready, sel_done, after))
                    if marks:
                        tn.wait_ge(sm_v.h, marks[-1])
                    it = tn.matmul(ps[:, 3072:3072 + 512],
                                   e_sel[0:8, e * 128:(e + 1) * 128],
                                   ct16[0:8, j * 512:(j + 1) * 512],
                                   start=True, stop=True)
                    sm_t.inc(it)
                tm = sm_t.v

                @block.vector
                def _(v):
                    v.wait_ge(sm_t.h, tm)
                    it = v.tensor_copy(cbc[:, slot * S + j * 512:
                                           slot * S + (j + 1) * 512],
                                       ps[:, 3072:3072 + 512])
                    sm_v.inc(it)
                marks.append(sm_v.v)
            return sm_v.v

        def phase_scatter(call, src_dram, dst_dram, dst_is_state,
                          cbc, P_b, y_pair, cbc_ready, y_ready):
            pb_marks = []
            sc_ev = []
            store_marks = []
            entry_snap = dqx.snapshot()
            for c in range(NCH):
                buf = c % 2
                if c % 2 == 0:
                    @block.vector
                    def _(v):
                        v.wait_ge(sm_v.h, cbc_ready)
                        v.drain()
                        if pb_marks:
                            v.wait_ge(sm_t.h, pb_marks[-1][1])
                        for si in range(2):
                            for rc in range(RC):
                                it = v.tensor_scalar(
                                    P_b[:, (si * RC + rc) * 256:
                                        (si * RC + rc + 1) * 256],
                                    cbc[:, si * S + (c // 2) * 256:
                                        si * S + (c // 2 + 1) * 256],
                                    iota_pr[:, rc:rc + 1],
                                    None, Alu.is_equal)
                        sm_v.inc(it)
                    pb_marks.append([sm_v.v, 0])
                pbm = pb_marks[-1][0]

                xm_box = []

                @block.sync
                def _(s):
                    if c < 2:
                        DmaQ.wait(s, entry_snap)
                    if len(store_marks) >= 2:
                        DmaQ.wait(s, store_marks[-2])
                    if len(sc_ev) >= 2:
                        s.wait_ge(sm_v.h, sc_ev[-2])
                    if src_dram is x_state:
                        DmaQ.wait(s, xstate_mark[c])
                    xm_box.append(dqx.dma(s, xch[:, buf * 1024:(buf + 1) * 1024],
                                          src_dram[c * 128:(c + 1) * 128, :]))

                @block.tensor
                def _(tn):
                    tn.wait_ge(sm_v.h, pbm)
                    if c == 0:
                        tn.wait_ge(sm_v.h, y_ready)
                    if len(sc_ev) >= 2:
                        tn.wait_ge(sm_v.h, sc_ev[-2])
                    pofs = (c % 2) * 2048
                    for si in range(2):
                        for rc in range(RC):
                            for dh in range(2):
                                it = tn.matmul(
                                    ps[:, pofs + dh * 512:pofs + (dh + 1) * 512],
                                    P_b[:, (si * RC + rc) * 256 + (c % 2) * 128:
                                        (si * RC + rc) * 256 + (c % 2) * 128 + 128],
                                    y_pair[:, si * RC * 1024 + rc * 1024 + dh * 512:
                                           si * RC * 1024 + rc * 1024 + (dh + 1) * 512],
                                    start=(si == 0 and rc == 0),
                                    stop=(si == 1 and rc == RC - 1))
                    sm_t.inc(it)
                pb_marks[-1][1] = sm_t.v
                scm = sm_t.v

                @block.vector
                def _(v):
                    v.wait_ge(sm_t.h, scm)
                    DmaQ.wait(v, xm_box[0])
                    pofs = (c % 2) * 2048
                    it = v.tensor_add(xch[:, buf * 1024:buf * 1024 + 512],
                                      xch[:, buf * 1024:buf * 1024 + 512],
                                      ps[:, pofs:pofs + 512])
                    it = v.tensor_add(xch[:, buf * 1024 + 512:buf * 1024 + 1024],
                                      xch[:, buf * 1024 + 512:buf * 1024 + 1024],
                                      ps[:, pofs + 512:pofs + 1024])
                    sm_v.inc(it)
                sc_ev.append(sm_v.v)
                evm = sm_v.v

                @block.sync
                def _(s):
                    s.wait_ge(sm_v.h, evm)
                    mk = dqx.dma(s, dst_dram[c * 128:(c + 1) * 128, :],
                                 xch[:, buf * 1024:(buf + 1) * 1024])
                    store_marks.append(mk)
                    if dst_is_state:
                        xstate_mark[c] = mk

        def phase_attention(att):
            vst, wio_sb, xl2_ch, xc_s = (att["vst"], att["wio_sb"], att["xl2_ch"],
                                         att["xc_s"])
            probsT = att["probsT"]
            dist0a = xc_s[:, 0:384]
            prfa = xc_s[:, 384:768]

            tv0, vv0 = sm_t.v, sm_v.v

            @block.gpsimd
            def _(g):
                g.wait_ge(sm_t.h, tv0)
                g.wait_ge(sm_v.h, vv0)
                it = g.iota(dist0a, pattern=[[128, 3], [-1, 128]], base=-128,
                            channel_multiplier=1,
                            allow_small_or_imprecise_dtypes=True)
                sm_gc.inc(it)
            gcm = sm_gc.v

            for h in range(NH):
                std, off = ATTN_STD[h], ATTN_OFF[h]

                @block.vector
                def _(v, h=h, std=std, off=off):
                    if h == 0:
                        v.wait_ge(sm_gc.h, gcm)
                    v.drain()
                    it = v.tensor_scalar(prfa, dist0a, -off, 1.0 / std,
                                         Alu.add, Alu.mult)
                    v.drain()
                    it = v.tensor_mul(prfa, prfa, prfa)
                    sm_v.inc(it)
                vmk = sm_v.v

                @block.scalar
                def _(s, h=h):
                    s.wait_ge(sm_v.h, vmk)
                    it = s.activation(probsT[:, h * 384:(h + 1) * 384], prfa,
                                      Act.Exp, bias=0.0, scale=-0.5)
                    sm_s.inc(it)
                smk = sm_s.v

                @block.vector
                def _(v2, h=h, std=std):
                    v2.wait_ge(sm_s.h, smk)
                    inv_n = 1.0 / (std * math.sqrt(2.0 * math.pi))
                    it = v2.tensor_scalar(probsT[:, h * 384:(h + 1) * 384],
                                          probsT[:, h * 384:(h + 1) * 384],
                                          inv_n, None, Alu.mult)
                    sm_v.inc(it)
            pb_done = sm_v.v

            @block.vector
            def _(v):
                v.drain()

            wim_box = []

            @block.sync
            def _(s):
                DmaQ.wait(s, marks["wi_cast"])
                wim_box.append(dqx.dma(
                    s, wio_sb[:], wi_bf.rearrange("(dc p) o -> p dc o", p=128)))

            xl2_done = []
            vsl_ev = []
            vmm_m = []
            vps_ev = []
            ventry = dqx.snapshot()
            for c in range(NCH):
                buf = c % 2

                xm_box = []

                @block.sync
                def _(s):
                    if c < 2:
                        DmaQ.wait(s, ventry)
                    if c >= 2:
                        s.wait_ge(sm_v.h, xl2_done[c - 2])
                    DmaQ.wait(s, xstate_mark[c])
                    xm_box.append(dqx.dma(s, xch[:, buf * 1024:(buf + 1) * 1024],
                                          x_state[c * 128:(c + 1) * 128, :]))

                @block.vector
                def _(v):
                    DmaQ.wait(v, xm_box[0])
                xl2_done.append(
                    emit_ln_chunk(xch[:, buf * 1024:(buf + 1) * 1024],
                                  xl2_ch[:, buf * 1024:(buf + 1) * 1024]))
                lnm = xl2_done[-1]

                @block.tensor
                def _(tn):
                    tn.wait_ge(sm_v.h, lnm)
                    if vsl_ev:
                        tn.wait_ge(sm_v.h, vsl_ev[-1])
                    for dcc in range(DC):
                        it = tn.transpose(psb[:, dcc * 128:(dcc + 1) * 128],
                                          xl2_ch[:, buf * 1024 + dcc * 128:
                                              buf * 1024 + (dcc + 1) * 128],
                                          ident_b[:])
                    sm_t.inc(it)
                tm = sm_t.v

                @block.vector
                def _(v):
                    v.wait_ge(sm_t.h, tm)
                    if len(vmm_m) >= 2:
                        v.wait_ge(sm_t.h, vmm_m[-2])
                    it = v.tensor_copy(tsl[:, buf * 1024:(buf + 1) * 1024],
                                       psb[:, 0:1024])
                    sm_v.inc(it)
                vsl_ev.append(sm_v.v)
                evm = sm_v.v

                @block.tensor
                def _(tn):
                    tn.wait_ge(sm_v.h, evm)
                    DmaQ.wait(tn, wim_box[0])
                    if len(vps_ev) >= 2:
                        tn.wait_ge(sm_v.h, vps_ev[-2])
                    pofs = 1024 + (c % 2) * 1024
                    for nh2 in range(2):
                        for dcc in range(DC):
                            it = tn.matmul(
                                ps[:, pofs + nh2 * 512:pofs + (nh2 + 1) * 512],
                                tsl[:, buf * 1024 + dcc * 128:
                                    buf * 1024 + (dcc + 1) * 128],
                                wio_sb[:, dcc * 1024 + nh2 * 512:
                                       dcc * 1024 + (nh2 + 1) * 512],
                                start=(dcc == 0), stop=(dcc == DC - 1))
                    sm_t.inc(it)
                vmm_m.append(sm_t.v)
                vm = sm_t.v

                @block.vector
                def _(v):
                    v.wait_ge(sm_t.h, vm)
                    pofs = 1024 + (c % 2) * 1024
                    it = v.tensor_copy(vst[:, c * 1024:(c + 1) * 1024],
                                       ps[:, pofs:pofs + 1024])
                    sm_v.inc(it)
                vps_ev.append(sm_v.v)
            v_done = sm_v.v

            wom_box = []

            @block.sync
            def _(s):
                s.wait_ge(sm_t.h, vmm_m[-1])
                DmaQ.wait(s, marks["wo_cast"])
                wom_box.append(dqx.dma(
                    s, wio_sb[:], wo_bf.rearrange("(dc p) o -> p dc o", p=128)))

            att_ev = []
            att_tr = []
            atl_ev = []
            omm_m = []
            ops_ev = []
            oln_done = []
            ocp_ev = []
            store_marks = []
            oentry = dqx.snapshot()
            for c in range(NCH):
                buf = c % 2
                pofs = (c % 2) * 1024

                @block.tensor
                def _(tn):
                    tn.wait_ge(sm_v.h, v_done)
                    if len(att_ev) >= 2:
                        tn.wait_ge(sm_v.h, att_ev[-2])
                    js = [j for j in (-1, 0, 1) if 0 <= c + j < NCH]
                    for hh in range(NH):
                        for ji, j in enumerate(js):
                            it = tn.matmul(
                                ps[:, pofs + hh * 128:pofs + (hh + 1) * 128],
                                probsT[:, (hh * 3 + j + 1) * 128:
                                       (hh * 3 + j + 2) * 128],
                                vst[:, (c + j) * 1024 + hh * 128:
                                    (c + j) * 1024 + (hh + 1) * 128],
                                start=(ji == 0), stop=(ji == len(js) - 1))
                    sm_t.inc(it)
                am = sm_t.v

                @block.vector
                def _(v):
                    v.wait_ge(sm_t.h, am)
                    if len(att_tr) >= 2:
                        v.wait_ge(sm_t.h, att_tr[-2])
                    it = v.tensor_copy(att_bf[:, buf * 1024:(buf + 1) * 1024],
                                       ps[:, pofs:pofs + 1024])
                    sm_v.inc(it)
                att_ev.append(sm_v.v)
                aem = sm_v.v

                @block.tensor
                def _(tn):
                    tn.wait_ge(sm_v.h, aem)
                    if atl_ev:
                        tn.wait_ge(sm_v.h, atl_ev[-1])
                    bofs = 4096 + (c % 2) * 1024
                    for dcc in range(DC):
                        it = tn.transpose(psb[:, bofs + dcc * 128:
                                              bofs + (dcc + 1) * 128],
                                          att_bf[:, buf * 1024 + dcc * 128:
                                              buf * 1024 + (dcc + 1) * 128],
                                          ident_b[:])
                    sm_t.inc(it)
                att_tr.append(sm_t.v)
                atm = sm_t.v

                @block.vector
                def _(v):
                    v.wait_ge(sm_t.h, atm)
                    if len(omm_m) >= 2:
                        v.wait_ge(sm_t.h, omm_m[-2])
                    bofs = 4096 + (c % 2) * 1024
                    it = v.tensor_copy(tsl[:, buf * 1024:(buf + 1) * 1024],
                                       psb[:, bofs:bofs + 1024])
                    sm_v.inc(it)
                atl_ev.append(sm_v.v)
                aslm = sm_v.v

                xm_box = []

                @block.sync
                def _(s):
                    if c < 2:
                        DmaQ.wait(s, oentry)
                        s.wait_ge(sm_v.h, xl2_done[-1])
                    if len(store_marks) >= 2:
                        DmaQ.wait(s, store_marks[-2])
                    if len(ocp_ev) >= 2:
                        s.wait_ge(sm_v.h, ocp_ev[-2])
                    DmaQ.wait(s, xstate_mark[c])
                    xm_box.append(dqx.dma(s, xch[:, buf * 1024:(buf + 1) * 1024],
                                          x_state[c * 128:(c + 1) * 128, :]))

                @block.tensor
                def _(tn):
                    tn.wait_ge(sm_v.h, aslm)
                    DmaQ.wait(tn, wom_box[0])
                    if len(ops_ev) >= 2:
                        tn.wait_ge(sm_v.h, ops_ev[-2])
                    tofs = 2048 + (c % 2) * 1024
                    for nh2 in range(2):
                        for dcc in range(DC):
                            it = tn.matmul(
                                ps[:, tofs + nh2 * 512:tofs + (nh2 + 1) * 512],
                                tsl[:, buf * 1024 + dcc * 128:
                                    buf * 1024 + (dcc + 1) * 128],
                                wio_sb[:, dcc * 1024 + nh2 * 512:
                                       dcc * 1024 + (nh2 + 1) * 512],
                                start=(dcc == 0), stop=(dcc == DC - 1))
                    sm_t.inc(it)
                omm_m.append(sm_t.v)
                om = sm_t.v

                @block.vector
                def _(v):
                    v.wait_ge(sm_t.h, om)
                    DmaQ.wait(v, xm_box[0])
                    tofs = 2048 + (c % 2) * 1024
                    it = v.tensor_add(xch[:, buf * 1024:buf * 1024 + 512],
                                      xch[:, buf * 1024:buf * 1024 + 512],
                                      ps[:, tofs:tofs + 512])
                    it = v.tensor_add(xch[:, buf * 1024 + 512:buf * 1024 + 1024],
                                      xch[:, buf * 1024 + 512:buf * 1024 + 1024],
                                      ps[:, tofs + 512:tofs + 1024])
                    sm_v.inc(it)
                ops_ev.append(sm_v.v)
                adm = sm_v.v

                @block.sync
                def _(s):
                    s.wait_ge(sm_v.h, adm)
                    a_dst = out if cfg.stop_after == "attn" else x_state
                    mk = dqx.dma(s, a_dst[c * 128:(c + 1) * 128, :],
                                 xch[:, buf * 1024:(buf + 1) * 1024])
                    store_marks.append(mk)
                    xstate_mark[c] = mk

                @block.vector
                def _(v):
                    v.drain()
                    it = v.tensor_copy(xc_s[:], xch[:, buf * 1024:(buf + 1) * 1024])
                    sm_v.inc(it)
                ocp_ev.append(sm_v.v)
                oln_done.append(emit_ln_chunk(xc_s[:], xln[:, c * 1024:(c + 1) * 1024]))

        for call in (1, 2):
            sel_es = ExitStack()
            with sel_es:
                aff_T = sel_es.enter_context(
                    nc.sbuf_tensor(f"aff_T{call}", [8, S], F32))
                mask_T = sel_es.enter_context(
                    nc.sbuf_tensor(f"mask_T{call}", [8, S], F32))
                ct32 = sel_es.enter_context(
                    nc.sbuf_tensor(f"ct32_{call}", [8, S], F32))
                bis_junk = sel_es.enter_context(
                    nc.sbuf_tensor(f"bj{call}", [8, S], FP16))
                lo_t = sel_es.enter_context(nc.sbuf_tensor(f"lo{call}", [8, 1], F32))
                gi_t = sel_es.enter_context(nc.sbuf_tensor(f"gi{call}", [8, 1], U8))
                hi_t = sel_es.enter_context(nc.sbuf_tensor(f"hi{call}", [8, 1], F32))
                mid_t = sel_es.enter_context(nc.sbuf_tensor(f"mid{call}", [8, 1], F32))
                cnt_t = sel_es.enter_context(nc.sbuf_tensor(f"cnt{call}", [8, 1], F32))
                ge_t = sel_es.enter_context(nc.sbuf_tensor(f"ge{call}", [8, 1], U8))

                barrier()
                phase_ln_router(call, x_in if call == 1 else None, aff_T)
                barrier()
                sel_done = phase_selection(
                    call, (aff_T, mask_T, ct32, bis_junk, lo_t, hi_t, mid_t,
                           cnt_t, ge_t, gi_t))

            exp_es = ExitStack()
            with exp_es:
                x_inT = exp_es.enter_context(
                    nc.sbuf_tensor(f"x_inT{call}", [128, DC * K], BF16))
                y_pair = exp_es.enter_context(
                    nc.sbuf_tensor(f"y_pair{call}", [128, 2 * RC * 1024], BF16))
                PT_b = exp_es.enter_context(
                    nc.sbuf_tensor(f"PT_b{call}", [128, 2 * 512], BF16))

                st = dict(x_inT=x_inT, y_pair=y_pair, PT_b=PT_b,
                          pt_cons=[], gath_ev=[], down_ev=[],
                          wgu_dma=[], wgu_marks=[], wd_cons=[])
                for pair in range(E // 2):
                    e1, e2 = 2 * pair, 2 * pair + 1
                    barrier()
                    work_es = ExitStack()
                    with work_es:
                        st["hT"] = work_es.enter_context(
                            nc.sbuf_tensor(f"hT{call}_{pair}", [128, NICH * RH],
                                           BF16))
                        st["wgu_sl"] = work_es.enter_context(
                            nc.sbuf_tensor(f"wgu{call}_{pair}", [128, 2 * 4096],
                                           BF16))
                        st["wd_sl"] = work_es.enter_context(
                            nc.sbuf_tensor(f"wd{call}_{pair}", [128, 2 * 2048],
                                           BF16))
                        phase_expert(call, e1, 0, st, sel_done)
                        phase_expert(call, e2, 1, st, sel_done)
                        y_ready = sm_v.v
                    barrier()
                    sc_es = ExitStack()
                    with sc_es:
                        cbc = sc_es.enter_context(
                            nc.sbuf_tensor(f"cbc{call}_{pair}", [128, 2 * S], FP16))
                        P_b = sc_es.enter_context(
                            nc.sbuf_tensor(f"P_b{call}_{pair}", [128, 16 * 256],
                                           BF16))
                        cm1 = emit_cbc(e1, 0, cbc, y_ready, sel_done)
                        cbc_ready = emit_cbc(e2, 1, cbc, y_ready, sel_done,
                                             after=cm1)
                        psrc = (x_in if (call == 1 and pair == 0) else x_state)
                        last = (pair == E // 2 - 1)
                        stop1 = (cfg.stop_after == "ffn1" and call == 1)
                        pdst = out if (last and (call == 2 or stop1)) else x_state
                        phase_scatter(call, psrc, pdst, not (call == 2 and last),
                                      cbc, P_b, y_pair, cbc_ready, y_ready)

            if cfg.stop_after == "ffn1":
                break
            if call == 1:
                barrier()
                att_es = ExitStack()
                with att_es:
                    att = dict(
                        vst=att_es.enter_context(
                            nc.sbuf_tensor("vst", [128, NCH * 1024], BF16)),
                        wio_sb=att_es.enter_context(
                            nc.sbuf_tensor("wio_sb", [128, DC * 1024], BF16)),
                        xl2_ch=att_es.enter_context(
                            nc.sbuf_tensor("xl2_ch", [128, 2 * 1024], BF16)),
                        xc_s=att_es.enter_context(
                            nc.sbuf_tensor("xc_s", [128, 1024], F32)),
                        probsT=att_es.enter_context(
                            nc.sbuf_tensor("probsT", [128, NH * 3 * 128], BF16)))
                    att_bf = att_es.enter_context(
                        nc.sbuf_tensor("att_bf", [128, 2 * 1024], BF16))
                    phase_attention(att)
                barrier()
            if cfg.stop_after == "attn":
                break

        @block.sync
        def _(s):
            dqx.wait_all(s)
            dqg.wait_all(s)

    return nc


def kernel(L0_attn_values, L0_attn_input_weights, L0_attn_output_weight,
           L0_ffn_gate_w, L0_ffn_w_gate, L0_ffn_w_up, L0_ffn_w_down):
    cfg = Cfg()
    B = L0_attn_values.shape[0]
    x = np.ascontiguousarray(np.asarray(L0_attn_values, dtype=np.float32))
    wgT = np.ascontiguousarray(np.asarray(L0_ffn_w_gate, np.float32).transpose(0, 2, 1))
    wuT = np.ascontiguousarray(np.asarray(L0_ffn_w_up, np.float32).transpose(0, 2, 1))
    wdT = np.ascontiguousarray(np.asarray(L0_ffn_w_down, np.float32).transpose(0, 2, 1))
    gwT = np.ascontiguousarray(np.asarray(L0_ffn_gate_w, np.float32).T)
    wiT = np.ascontiguousarray(np.asarray(L0_attn_input_weights, np.float32).T)
    woT = np.ascontiguousarray(np.asarray(L0_attn_output_weight, np.float32).T)

    nc = build_program(cfg)
    in_maps = []
    for b in range(B):
        in_maps.append(dict(x_in=np.ascontiguousarray(x[b]), wgT=wgT, wuT=wuT,
                            wdT=wdT, gwT=gwT, wiT=wiT, woT=woT))
    res = run_bass_kernel_spmd(nc, in_maps, core_ids=list(range(B)))
    return np.stack([res.results[b]["out"] for b in range(B)], axis=0)


# version 12 (final)
# speedup vs baseline: 1.0500x; 1.0109x over previous
import math
from contextlib import ExitStack

import numpy as np
import ml_dtypes

import concourse.bass as bass
import concourse.mybir as mybir
from concourse.bass_utils import run_bass_kernel_spmd

F32 = mybir.dt.float32
U8 = mybir.dt.uint8
BF16 = mybir.dt.bfloat16
FP16 = mybir.dt.float16
Act = mybir.ActivationFunctionType
Alu = mybir.AluOpType
AX = mybir.AxisListType.X

ATTN_STD = [1.0, 2.0, 4.0, 8.0, 1.0, 2.0, 4.0, 8.0]
ATTN_OFF = [-1.0, -2.0, -4.0, -8.0, -1.0, -2.0, -4.0, -8.0]


class Cfg:
    def __init__(self, nch=32, nich=32, bisect=36, stop_after=None):
        self.stop_after = stop_after
        self.D = 1024
        self.DC = 8
        self.NCH = nch
        self.S = 128 * nch
        self.NICH = nich
        self.I = 128 * nich
        self.E = 8
        self.K = self.S // self.E * 2
        self.RC = self.K // 128
        self.RH = self.K // 2
        self.NH = 8
        self.BISECT = bisect


class Sem:
    def __init__(self, es, nc, name):
        self.h = es.enter_context(nc.semaphore(name))
        self.v = 0

    def inc(self, instr, n=1):
        instr.then_inc(self.h, n)
        self.v += n
        return self.v


class DmaQ:
    def __init__(self, es, nc, name, n):
        self.sems = [Sem(es, nc, f"{name}{i}") for i in range(n)]
        self.i = 0

    def dma(self, eng, out, in_):
        s = self.sems[self.i % len(self.sems)]
        eng.wait_ge(s.h, s.v)
        it = eng.dma_start(out=out, in_=in_)
        s.inc(it, 16)
        self.i += 1
        return (s, s.v)

    @staticmethod
    def wait(eng, mark):
        if mark is None:
            return
        if isinstance(mark, list):
            for m in mark:
                eng.wait_ge(m[0].h, m[1])
        else:
            eng.wait_ge(mark[0].h, mark[1])

    def snapshot(self):
        return [(s, s.v) for s in self.sems]

    def wait_all(self, eng):
        for s in self.sems:
            eng.wait_ge(s.h, s.v)


def build_program(cfg: Cfg):
    nc = bass.Bass("TRN2", target_bir_lowering=False)
    D, DC, NCH, E, K, RC, RH = cfg.D, cfg.DC, cfg.NCH, cfg.E, cfg.K, cfg.RC, cfg.RH
    S, I, NICH, NH = cfg.S, cfg.I, cfg.NICH, cfg.NH

    x_in = nc.dram_tensor("x_in", [S, D], F32, kind="ExternalInput").ap()
    wgT = nc.dram_tensor("wgT", [E, D, I], F32, kind="ExternalInput").ap()
    wuT = nc.dram_tensor("wuT", [E, D, I], F32, kind="ExternalInput").ap()
    wdT = nc.dram_tensor("wdT", [E, I, D], F32, kind="ExternalInput").ap()
    gwT = nc.dram_tensor("gwT", [D, E], F32, kind="ExternalInput").ap()
    wiT = nc.dram_tensor("wiT", [D, D], F32, kind="ExternalInput").ap()
    woT = nc.dram_tensor("woT", [D, D], F32, kind="ExternalInput").ap()
    out = nc.dram_tensor("out", [S, D], F32, kind="ExternalOutput").ap()
    x_state = nc.dram_tensor("x_state", [S, D], F32, kind="Internal").ap()
    wg_bf = nc.dram_tensor("wg_bf", [E, D, I], BF16, kind="Internal").ap()
    wu_bf = nc.dram_tensor("wu_bf", [E, D, I], BF16, kind="Internal").ap()
    wd_bf = nc.dram_tensor("wd_bf", [E, I, D], BF16, kind="Internal").ap()
    wi_bf = nc.dram_tensor("wi_bf", [D, D], BF16, kind="Internal").ap()
    wo_bf = nc.dram_tensor("wo_bf", [D, D], BF16, kind="Internal").ap()
    gw_bf = nc.dram_tensor("gw_bf", [D, E], BF16, kind="Internal").ap()

    es = ExitStack()
    sb = lambda name, shape, dt: es.enter_context(nc.sbuf_tensor(name, shape, dt))

    with es:
        xln = sb("xln", [128, NCH * 1024], BF16)
        tsl = sb("tsl", [128, 2 * 1024], BF16)
        xch = sb("xch", [128, 2 * 1024], F32)
        a_cols = sb("a_cols", [128, NCH * E], F32)
        a_cols_bf = sb("a_cols_bf", [128, NCH * E], BF16)
        ct16 = sb("ct16", [8, S], FP16)
        ct_cols = sb("ct_cols", [128, NCH * E], F32)
        iota_r = sb("iota_r", [128, 1024], FP16)
        iota_pr = sb("iota_pr", [128, RC], F32)
        ident_b = sb("ident_b", [128, 128], BF16)
        ident_f = sb("ident_f", [128, 128], F32)
        ones16 = sb("ones16", [1, 128], FP16)
        e_sel = sb("e_sel", [8, 8 * 128], FP16)
        gw_sb = sb("gw_sb", [128, DC * E], BF16)
        g_cols = sb("g_cols", [128, 2 * RC], F32)
        g_row = sb("g_row", [1, K], F32)
        mneg = sb("mneg", [128, 1], F32)
        lnt = sb("lnt", [128, 1], F32)
        vsum = sb("vsum", [128, 1], F32)
        rstd = sb("rstd", [128, 1], F32)
        negmx = sb("negmx", [128, 1], F32)
        ssum = sb("ssum", [128, 1], F32)
        srcp = sb("srcp", [128, 1], F32)
        aexp = sb("aexp", [128, E], F32)
        sil_t = sb("sil_t", [128, 2 * 512], F32)
        sq_junk = sil_t[:].bitcast(BF16)[:, 0:1024]
        const_sb = sb("const_sb", [128, 2], F32)
        nc.const_aps.register(F32, 0.0, const_sb[:, 0:1])
        nc.const_aps.register(F32, 1e-5, const_sb[:, 1:2])

        dqx = DmaQ(es, nc, "dqx", 8)
        dqg = DmaQ(es, nc, "dqg", 2)
        sm_gc = Sem(es, nc, "sm_gc")
        sm_t = Sem(es, nc, "sm_t")
        sm_v = Sem(es, nc, "sm_v")
        sm_s = Sem(es, nc, "sm_s")
        block = es.enter_context(nc.Block())
        ps = es.enter_context(nc.psum_tensor("ps", [128, 4096], F32))
        psb = ps[:].bitcast(BF16)

        xstate_mark = [None] * NCH
        marks = {}
        dist0 = xch[:, 0:384]
        prf = xch[:, 384:768]
        e_sel_i = xch[0:8, 768:1792]

        def barrier():
            tv, vv, sv = sm_t.v, sm_v.v, sm_s.v

            @block.tensor
            def _(tn):
                tn.wait_ge(sm_v.h, vv)
                tn.wait_ge(sm_s.h, sv)

            @block.vector
            def _(v):
                v.wait_ge(sm_t.h, tv)
                v.wait_ge(sm_s.h, sv)
                v.drain()

            @block.scalar
            def _(s):
                s.wait_ge(sm_t.h, tv)
                s.wait_ge(sm_v.h, vv)
                s.drain()

        @block.gpsimd
        def _(g):
            it = g.iota(iota_r[:], pattern=[[1, 1024]], base=0, channel_multiplier=0,
                        allow_small_or_imprecise_dtypes=True)
            it = g.iota(iota_pr[:], pattern=[[128, RC]], base=0, channel_multiplier=1,
                        allow_small_or_imprecise_dtypes=True)
            it = g.iota(prf[:, 0:128], pattern=[[1, 128]], base=0,
                        channel_multiplier=-1, allow_small_or_imprecise_dtypes=True)
            it = g.iota(e_sel_i, pattern=[[-1, 8], [0, 128]], base=0,
                        channel_multiplier=1, allow_small_or_imprecise_dtypes=True)
            it = g.memset(ones16[:], 1.0)
            sm_gc.inc(it, 5)
            marks["gw_cast"] = dqg.dma(g, gw_bf, gwT)
            marks["wi_cast"] = dqg.dma(g, wi_bf, wiT)
            marks["wo_cast"] = dqg.dma(g, wo_bf, woT)

        cast_done = {}

        @block.gpsimd
        def _(g):
            for e in range(E):
                dqg.dma(g, wg_bf[e], wgT[e])
                dqg.dma(g, wu_bf[e], wuT[e])
                dqg.dma(g, wd_bf[e], wdT[e])
                cast_done[e] = dqg.snapshot()

        @block.vector
        def _(v):
            v.memset(const_sb[:, 0:1], 0.0)
            v.memset(const_sb[:, 1:2], 1e-5)
            v.drain()
            v.wait_ge(sm_gc.h, 5)
            it = v.tensor_scalar(e_sel[:], e_sel_i, 0.0, None, Alu.is_equal)
            it = v.tensor_scalar(ident_b[:], prf[:, 0:128], 0.0, None, Alu.is_equal)
            it = v.tensor_scalar(ident_f[:], prf[:, 0:128], 0.0, None, Alu.is_equal)
            sm_v.inc(it)

        marks["const_vm"] = sm_v.v

        @block.sync
        def _(s):
            DmaQ.wait(s, marks["gw_cast"])
            marks["gw_sb"] = dqx.dma(s, gw_sb[:],
                                     gw_bf.rearrange("(c p) e -> p c e", p=128))

        def emit_ln_chunk(src_ap, dst_ap):
            @block.vector
            def _(v):
                v.drain()
                it = v.tensor_reduce(mneg[:], src_ap, AX, Alu.add, negate=True)
                v.drain()
                it = v.tensor_scalar(lnt[:], mneg[:], 1.0 / D, None, Alu.mult)
                v.drain()
                it = v.tensor_scalar(src_ap, src_ap, lnt[:, 0:1], None, Alu.add)
                v.drain()
                it = v.scalar_tensor_tensor(sq_junk, src_ap, 1.0, src_ap,
                                            Alu.mult, Alu.mult, accum_out=vsum[:])
                sm_v.inc(it)
            m1 = sm_v.v

            @block.scalar
            def _(s):
                s.wait_ge(sm_v.h, m1)
                it = s.activation(rstd[:], vsum[:], Act.Ln, bias=1e-5, scale=1.0 / D)
                s.drain()
                it = s.activation(rstd[:], rstd[:], Act.Exp, bias=0.0, scale=-0.5)
                sm_s.inc(it)
            m2 = sm_s.v

            @block.vector
            def _(v):
                v.wait_ge(sm_s.h, m2)
                it = v.tensor_scalar(dst_ap, src_ap, rstd[:, 0:1], None, Alu.mult)
                sm_v.inc(it)
            return sm_v.v

        def phase_ln_router(call, src_dram, aff_T):
            slab_ev = []
            router_m = []
            smx_cons = []
            ln_done = []
            aT_ev = []
            for c in range(NCH):
                buf = c % 2
                if src_dram is not None:
                    xm_box = []

                    @block.sync
                    def _(s):
                        if c < 2:
                            s.wait_ge(sm_v.h, marks["const_vm"])
                        if c >= 2:
                            s.wait_ge(sm_v.h, ln_done[c - 2])
                        xm_box.append(
                            dqx.dma(s, xch[:, buf * 1024:(buf + 1) * 1024],
                                    src_dram[c * 128:(c + 1) * 128, :]))

                    @block.vector
                    def _(v):
                        DmaQ.wait(v, xm_box[0])
                    ln_done.append(
                        emit_ln_chunk(xch[:, buf * 1024:(buf + 1) * 1024],
                                      xln[:, c * 1024:(c + 1) * 1024]))
                lnm = sm_v.v

                @block.tensor
                def _(tn):
                    tn.wait_ge(sm_v.h, lnm)
                    if slab_ev:
                        tn.wait_ge(sm_v.h, slab_ev[-1])
                    for dcc in range(DC):
                        it = tn.transpose(psb[:, dcc * 128:(dcc + 1) * 128],
                                          xln[:, c * 1024 + dcc * 128:
                                              c * 1024 + (dcc + 1) * 128],
                                          ident_b[:])
                    sm_t.inc(it)
                ttm = sm_t.v

                @block.vector
                def _(v):
                    v.wait_ge(sm_t.h, ttm)
                    if len(router_m) >= 2:
                        v.wait_ge(sm_t.h, router_m[-2])
                    it = v.tensor_copy(tsl[:, buf * 1024:(buf + 1) * 1024],
                                       psb[:, 0:1024])
                    sm_v.inc(it)
                slab_ev.append(sm_v.v)
                evm = sm_v.v

                @block.tensor
                def _(tn):
                    tn.wait_ge(sm_v.h, evm)
                    DmaQ.wait(tn, marks["gw_sb"])
                    if len(smx_cons) >= 2:
                        tn.wait_ge(sm_s.h, smx_cons[-2][0])
                        tn.wait_ge(sm_v.h, smx_cons[-2][1])
                    rbase = 1024 + (c % 2) * 512
                    for dcc in range(DC):
                        it = tn.matmul(ps[:, rbase:rbase + E],
                                       tsl[:, buf * 1024 + dcc * 128:
                                           buf * 1024 + (dcc + 1) * 128],
                                       gw_sb[:, dcc * E:(dcc + 1) * E],
                                       start=(dcc == 0), stop=(dcc == DC - 1))
                    sm_t.inc(it)
                router_m.append(sm_t.v)
                rm = sm_t.v

                @block.vector
                def _(v):
                    v.wait_ge(sm_t.h, rm)
                    rbase = 1024 + (c % 2) * 512
                    it = v.reduce_max(negmx[:], ps[:, rbase:rbase + E], AX,
                                      negate=True)
                    sm_v.inc(it)
                nm = sm_v.v

                @block.scalar
                def _(s):
                    s.wait_ge(sm_v.h, nm)
                    rbase = 1024 + (c % 2) * 512
                    it = s.activation(aexp[:], ps[:, rbase:rbase + E], Act.Exp,
                                      bias=negmx[:, 0:1], scale=1.0)
                    sm_s.inc(it)
                sx = sm_s.v

                @block.vector
                def _(v):
                    v.wait_ge(sm_s.h, sx)
                    it = v.tensor_reduce(ssum[:], aexp[:], AX, Alu.add)
                    v.drain()
                    it = v.reciprocal(srcp[:], ssum[:])
                    v.drain()
                    it = v.tensor_scalar(a_cols[:, c * E:(c + 1) * E], aexp[:],
                                         srcp[:, 0:1], 0.5, Alu.mult, Alu.mult)
                    sm_v.inc(it)
                smx_cons.append((sx, sm_v.v))
                am = sm_v.v

                @block.tensor
                def _(tn):
                    tn.wait_ge(sm_v.h, am)
                    if aT_ev:
                        tn.wait_ge(sm_v.h, aT_ev[-1])
                    it = tn.transpose(ps[0:E, 2048:2048 + 128],
                                      a_cols[:, c * E:(c + 1) * E], ident_f[:])
                    sm_t.inc(it)
                t2 = sm_t.v

                @block.vector
                def _(v):
                    v.wait_ge(sm_t.h, t2)
                    it = v.tensor_copy(aff_T[:, c * 128:(c + 1) * 128],
                                       ps[0:E, 2048:2048 + 128])
                    sm_v.inc(it)
                aT_ev.append(sm_v.v)

        def phase_selection(call, sel):
            aff_T, mask_T, ct32, bis_junk, lo_t, hi_t, mid_t, cnt_t, ge_t, gi_t = sel

            @block.vector
            def _(v):
                v.drain()
                it = v.tensor_copy(a_cols_bf[:], a_cols[:])
                v.drain()
                it = v.memset(lo_t[:], 0.0)
                it = v.memset(hi_t[:], 1.0)
                it = v.memset(mid_t[:], 0.5)
                v.drain()
                for _i in range(cfg.BISECT):
                    it = v.tensor_scalar(bis_junk[:], aff_T[:], mid_t[:, 0:1], None,
                                         Alu.is_ge, Alu.add, accum_out=cnt_t[:])
                    v.drain()
                    it = v.tensor_scalar(ge_t[:], cnt_t[:], float(K) - 0.5, None,
                                         Alu.is_ge)
                    it = v.tensor_scalar(gi_t[:], cnt_t[:], float(K) - 0.5, None,
                                         Alu.is_lt)
                    v.drain()
                    it = v.select(lo_t[:], ge_t[:], mid_t[:], lo_t[:],
                                  add_drain=True)
                    v.drain()
                    it = v.select(hi_t[:], gi_t[:], mid_t[:], hi_t[:],
                                  add_drain=True)
                    v.drain()
                    it = v.tensor_scalar(mid_t[:], lo_t[:], hi_t[:, 0:1], 0.5,
                                         Alu.add, Alu.mult)
                    v.drain()
                it = v.tensor_scalar(mask_T[:], aff_T[:], lo_t[:, 0:1], None,
                                     Alu.is_ge)
                v.drain()
                it = v.tensor_tensor_scan(ct32[:], mask_T[:], mask_T[:], 0.0,
                                          Alu.add, Alu.bypass)
                v.drain()
                it = v.tensor_mul(ct32[:], ct32[:], mask_T[:])
                v.drain()
                it = v.tensor_scalar(ct32[:], ct32[:], 1.0, None, Alu.subtract)
                v.drain()
                it = v.tensor_copy(ct16[:], ct32[:])
                sm_v.inc(it)
            selm = sm_v.v

            prev = []
            for c in range(NCH):
                @block.tensor
                def _(tn):
                    tn.wait_ge(sm_v.h, selm)
                    if len(prev) >= 2:
                        tn.wait_ge(sm_v.h, prev[-2])
                    pofs = 2048 + (c % 2) * 512
                    it = tn.transpose(ps[0:128, pofs:pofs + E],
                                      ct32[:, c * 128:(c + 1) * 128],
                                      ident_f[0:E, 0:E])
                    sm_t.inc(it)
                tm = sm_t.v

                @block.vector
                def _(v):
                    v.wait_ge(sm_t.h, tm)
                    pofs = 2048 + (c % 2) * 512
                    it = v.tensor_copy(ct_cols[:, c * E:(c + 1) * E],
                                       ps[0:128, pofs:pofs + E])
                    sm_v.inc(it)
                prev.append(sm_v.v)
            return sm_v.v

        def phase_expert(call, e, slot, st, sel_done):
            x_inT, y_pair, PT_b, hT, wgu_sl, wd_sl = (
                st["x_inT"], st["y_pair"], st["PT_b"], st["hT"], st["wgu_sl"],
                st["wd_sl"])
            pt_cons = st["pt_cons"]

            for h in range(2):
                for grp in range(2):
                    for c in range(NCH):
                        rslot = c % 4
                        if rslot < 2:
                            pt_ap = PT_b[:, rslot * RH:(rslot + 1) * RH]
                        else:
                            pt_ap = tsl[:, (rslot - 2) * RH:(rslot - 1) * RH]

                        @block.vector
                        def _(v):
                            if c == 0 and h == 0 and grp == 0:
                                v.wait_ge(sm_v.h, sel_done)
                                v.drain()
                            if len(pt_cons) >= 4:
                                v.wait_ge(sm_t.h, pt_cons[-4])
                            it = v.tensor_scalar(
                                pt_ap,
                                iota_r[:, h * RH:h * RH + RH],
                                ct_cols[:, c * E + e:c * E + e + 1],
                                None, Alu.is_equal)
                            sm_v.inc(it)
                        vm = sm_v.v

                        @block.tensor
                        def _(tn):
                            tn.wait_ge(sm_v.h, vm)
                            if c == 0:
                                if st["gath_ev"]:
                                    tn.wait_ge(sm_v.h, st["gath_ev"][-1])
                                if st["down_ev"]:
                                    tn.wait_ge(sm_v.h, st["down_ev"][-1])
                            for dd in range(4):
                                dcc = grp * 4 + dd
                                it = tn.matmul(
                                    ps[:, dd * 512:dd * 512 + RH],
                                    xln[:, c * 1024 + dcc * 128:
                                        c * 1024 + (dcc + 1) * 128],
                                    pt_ap,
                                    start=(c == 0), stop=(c == NCH - 1))
                            if grp == 0:
                                it = tn.matmul(
                                    ps[0:1, 2048:2048 + RH],
                                    a_cols_bf[:, c * E + e:c * E + e + 1],
                                    pt_ap,
                                    start=(c == 0), stop=(c == NCH - 1))
                            sm_t.inc(it)
                        pt_cons.append(sm_t.v)
                    gm = sm_t.v

                    @block.vector
                    def _(v):
                        v.wait_ge(sm_t.h, gm)
                        for dd in range(4):
                            dcc = grp * 4 + dd
                            it = v.tensor_copy(
                                x_inT[:, dcc * K + h * RH:dcc * K + h * RH + RH],
                                ps[:, dd * 512:dd * 512 + RH])
                        if grp == 0:
                            it = v.tensor_copy(g_row[0:1, h * RH:h * RH + RH],
                                               ps[0:1, 2048:2048 + RH])
                        sm_v.inc(it)
                    st["gath_ev"].append(sm_v.v)
            xin_ready = sm_v.v

            prev = []
            for j in range(RC):
                @block.tensor
                def _(tn):
                    tn.wait_ge(sm_v.h, xin_ready)
                    if len(prev) >= 2:
                        tn.wait_ge(sm_v.h, prev[-2])
                    pofs = 2560 + (j % 2)
                    it = tn.transpose(ps[:, pofs:pofs + 1],
                                      g_row[0:1, j * 128:(j + 1) * 128],
                                      ident_f[0:1, 0:1])
                    sm_t.inc(it)
                tj = sm_t.v

                @block.vector
                def _(v):
                    v.wait_ge(sm_t.h, tj)
                    pofs = 2560 + (j % 2)
                    it = v.tensor_copy(g_cols[:, slot * RC + j:slot * RC + j + 1],
                                       ps[:, pofs:pofs + 1])
                    sm_v.inc(it)
                prev.append(sm_v.v)
            g_ready = sm_v.v

            for h in range(2):
                gu_ev = []
                for ic in range(NICH):
                    if ic % 2 == 0:
                        @block.sync
                        def _(s):
                            DmaQ.wait(s, cast_done[e])
                            if len(st["wgu_marks"]) >= 3:
                                s.wait_ge(sm_t.h, st["wgu_marks"][-3])
                            i0 = ic * 128
                            bofs = (ic // 2 % 2) * 2048
                            m1_ = dqx.dma(s, wgu_sl[:, bofs:bofs + 2048],
                                          wg_bf[e][:, i0:i0 + 256].rearrange(
                                              "(dc p) i -> p dc i", p=128))
                            m2_ = dqx.dma(
                                s, wgu_sl[:, 4096 + bofs:4096 + bofs + 2048],
                                wu_bf[e][:, i0:i0 + 256].rearrange(
                                    "(dc p) i -> p dc i", p=128))
                            st["wgu_dma"].append([m1_, m2_])
                    wm = st["wgu_dma"][-1]

                    @block.tensor
                    def _(tn):
                        DmaQ.wait(tn, wm)
                        if ic == 0:
                            tn.wait_ge(sm_v.h, xin_ready)
                            if st["down_ev"]:
                                tn.wait_ge(sm_v.h, st["down_ev"][-1])
                        if len(gu_ev) >= 2:
                            tn.wait_ge(sm_v.h, gu_ev[-2])
                        base = (ic // 2 % 2) * 2048 + (ic % 2) * 128
                        pG = (ic % 2) * 512
                        pU = 1024 + (ic % 2) * 512
                        for dcc in range(DC):
                            it = tn.matmul(
                                ps[:, pG:pG + RH],
                                wgu_sl[:, base + dcc * 256:base + dcc * 256 + 128],
                                x_inT[:, dcc * K + h * RH:dcc * K + h * RH + RH],
                                start=(dcc == 0), stop=(dcc == DC - 1))
                            it = tn.matmul(
                                ps[:, pU:pU + RH],
                                wgu_sl[:, 4096 + base + dcc * 256:
                                       4096 + base + dcc * 256 + 128],
                                x_inT[:, dcc * K + h * RH:dcc * K + h * RH + RH],
                                start=(dcc == 0), stop=(dcc == DC - 1))
                        sm_t.inc(it)
                    st["wgu_marks"].append(sm_t.v)
                    gum = sm_t.v

                    @block.scalar
                    def _(s):
                        s.wait_ge(sm_t.h, gum)
                        if len(gu_ev) >= 2:
                            s.wait_ge(sm_v.h, gu_ev[-2])
                        pG = (ic % 2) * 512
                        so = (ic % 2) * 512
                        it = s.activation(sil_t[:, so:so + RH], ps[:, pG:pG + RH],
                                          Act.Silu, bias=0.0, scale=1.0)
                        sm_s.inc(it)
                    sm2 = sm_s.v

                    @block.vector
                    def _(v):
                        v.wait_ge(sm_s.h, sm2)
                        pU = 1024 + (ic % 2) * 512
                        so = (ic % 2) * 512
                        it = v.tensor_mul(hT[:, ic * RH:(ic + 1) * RH],
                                          sil_t[:, so:so + RH], ps[:, pU:pU + RH])
                        sm_v.inc(it)
                    gu_ev.append(sm_v.v)
                hT_done = sm_v.v

                for icg in range(NICH // 2):
                    wd_box = []

                    @block.sync
                    def _(s):
                        DmaQ.wait(s, cast_done[e])
                        if len(st["wd_cons"]) >= 2:
                            s.wait_ge(sm_t.h, st["wd_cons"][-2])
                        wd_box.append(dqx.dma(
                            s, wd_sl[:, (icg % 2) * 2048:(icg % 2) * 2048 + 2048],
                            wd_bf[e][icg * 256:(icg + 1) * 256, :].rearrange(
                                "(ig p) d2 -> p ig d2", p=128)))

                    @block.tensor
                    def _(tn):
                        DmaQ.wait(tn, wd_box[0])
                        if icg == 0:
                            tn.wait_ge(sm_v.h, hT_done)
                            tn.wait_ge(sm_v.h, g_ready)
                            if st["down_ev"]:
                                tn.wait_ge(sm_v.h, st["down_ev"][-1])
                        for ig in range(2):
                            ic = icg * 2 + ig
                            wofs = (icg % 2) * 2048 + ig * 1024
                            for r2 in range(RH // 128):
                                for dh in range(2):
                                    it = tn.matmul(
                                        ps[:, r2 * 1024 + dh * 512:
                                           r2 * 1024 + (dh + 1) * 512],
                                        hT[:, ic * RH + r2 * 128:
                                           ic * RH + (r2 + 1) * 128],
                                        wd_sl[:, wofs + dh * 512:
                                              wofs + (dh + 1) * 512],
                                        start=(ic == 0), stop=(ic == NICH - 1))
                        sm_t.inc(it)
                    st["wd_cons"].append(sm_t.v)
                dm = sm_t.v

                @block.vector
                def _(v):
                    v.wait_ge(sm_t.h, dm)
                    for r2 in range(RH // 128):
                        rc = h * (RH // 128) + r2
                        for dh in range(2):
                            it = v.tensor_scalar(
                                y_pair[:, slot * RC * 1024 + rc * 1024 + dh * 512:
                                       slot * RC * 1024 + rc * 1024 + (dh + 1) * 512],
                                ps[:, r2 * 1024 + dh * 512:
                                   r2 * 1024 + (dh + 1) * 512],
                                g_cols[:, slot * RC + rc:slot * RC + rc + 1],
                                None, Alu.mult)
                    sm_v.inc(it)
                st["down_ev"].append(sm_v.v)
            return sm_v.v

        def emit_cbc(e, slot, cbc, y_ready, sel_done, after=0):
            marks = []
            for j in range(S // 512):
                @block.tensor
                def _(tn):
                    if j == 0:
                        tn.wait_ge(sm_v.h, max(y_ready, sel_done, after))
                    if marks:
                        tn.wait_ge(sm_v.h, marks[-1])
                    it = tn.matmul(ps[:, 3072:3072 + 512],
                                   e_sel[0:8, e * 128:(e + 1) * 128],
                                   ct16[0:8, j * 512:(j + 1) * 512],
                                   start=True, stop=True)
                    sm_t.inc(it)
                tm = sm_t.v

                @block.vector
                def _(v):
                    v.wait_ge(sm_t.h, tm)
                    it = v.tensor_copy(cbc[:, slot * S + j * 512:
                                           slot * S + (j + 1) * 512],
                                       ps[:, 3072:3072 + 512])
                    sm_v.inc(it)
                marks.append(sm_v.v)
            return sm_v.v

        def phase_scatter(call, src_dram, dst_dram, dst_is_state,
                          cbc, P_b, P_b2, y_pair, cbc_ready, y_ready):
            pb_marks = []
            sc_ev = []
            store_marks = []
            entry_snap = dqx.snapshot()
            for c in range(NCH):
                buf = c % 2
                if c % 2 == 0:
                    Pg = P_b if (c // 2) % 2 == 0 else P_b2

                    @block.vector
                    def _(v):
                        v.wait_ge(sm_v.h, cbc_ready)
                        v.drain()
                        if len(pb_marks) >= 2:
                            v.wait_ge(sm_t.h, pb_marks[-2][1])
                        for si in range(2):
                            for rc in range(RC):
                                it = v.tensor_scalar(
                                    Pg[:, (si * RC + rc) * 256:
                                       (si * RC + rc + 1) * 256],
                                    cbc[:, si * S + (c // 2) * 256:
                                        si * S + (c // 2 + 1) * 256],
                                    iota_pr[:, rc:rc + 1],
                                    None, Alu.is_equal)
                        sm_v.inc(it)
                    pb_marks.append([sm_v.v, 0])
                pbm = pb_marks[-1][0]
                Pg = P_b if (c // 2) % 2 == 0 else P_b2

                xm_box = []

                @block.sync
                def _(s):
                    if c < 2:
                        DmaQ.wait(s, entry_snap)
                    if len(store_marks) >= 2:
                        DmaQ.wait(s, store_marks[-2])
                    if len(sc_ev) >= 2:
                        s.wait_ge(sm_v.h, sc_ev[-2])
                    if src_dram is x_state:
                        DmaQ.wait(s, xstate_mark[c])
                    xm_box.append(dqx.dma(s, xch[:, buf * 1024:(buf + 1) * 1024],
                                          src_dram[c * 128:(c + 1) * 128, :]))

                @block.tensor
                def _(tn):
                    tn.wait_ge(sm_v.h, pbm)
                    if c == 0:
                        tn.wait_ge(sm_v.h, y_ready)
                    if len(sc_ev) >= 2:
                        tn.wait_ge(sm_v.h, sc_ev[-2])
                    pofs = (c % 2) * 2048
                    for si in range(2):
                        for rc in range(RC):
                            for dh in range(2):
                                it = tn.matmul(
                                    ps[:, pofs + dh * 512:pofs + (dh + 1) * 512],
                                    Pg[:, (si * RC + rc) * 256 + (c % 2) * 128:
                                       (si * RC + rc) * 256 + (c % 2) * 128 + 128],
                                    y_pair[:, si * RC * 1024 + rc * 1024 + dh * 512:
                                           si * RC * 1024 + rc * 1024 + (dh + 1) * 512],
                                    start=(si == 0 and rc == 0),
                                    stop=(si == 1 and rc == RC - 1))
                    sm_t.inc(it)
                pb_marks[-1][1] = sm_t.v
                scm = sm_t.v

                @block.vector
                def _(v):
                    v.wait_ge(sm_t.h, scm)
                    DmaQ.wait(v, xm_box[0])
                    pofs = (c % 2) * 2048
                    it = v.tensor_add(xch[:, buf * 1024:buf * 1024 + 512],
                                      xch[:, buf * 1024:buf * 1024 + 512],
                                      ps[:, pofs:pofs + 512])
                    it = v.tensor_add(xch[:, buf * 1024 + 512:buf * 1024 + 1024],
                                      xch[:, buf * 1024 + 512:buf * 1024 + 1024],
                                      ps[:, pofs + 512:pofs + 1024])
                    sm_v.inc(it)
                sc_ev.append(sm_v.v)
                evm = sm_v.v

                @block.sync
                def _(s):
                    s.wait_ge(sm_v.h, evm)
                    mk = dqx.dma(s, dst_dram[c * 128:(c + 1) * 128, :],
                                 xch[:, buf * 1024:(buf + 1) * 1024])
                    store_marks.append(mk)
                    if dst_is_state:
                        xstate_mark[c] = mk

        def phase_attention(att):
            vst, wio_sb, xl2_ch, xc_s = (att["vst"], att["wio_sb"], att["xl2_ch"],
                                         att["xc_s"])
            probsT = att["probsT"]
            dist0a = xc_s[:, 0:384]
            prfa = xc_s[:, 384:768]

            tv0, vv0 = sm_t.v, sm_v.v

            @block.gpsimd
            def _(g):
                g.wait_ge(sm_t.h, tv0)
                g.wait_ge(sm_v.h, vv0)
                it = g.iota(dist0a, pattern=[[128, 3], [-1, 128]], base=-128,
                            channel_multiplier=1,
                            allow_small_or_imprecise_dtypes=True)
                sm_gc.inc(it)
            gcm = sm_gc.v

            for h in range(NH):
                std, off = ATTN_STD[h], ATTN_OFF[h]

                @block.vector
                def _(v, h=h, std=std, off=off):
                    if h == 0:
                        v.wait_ge(sm_gc.h, gcm)
                    v.drain()
                    it = v.tensor_scalar(prfa, dist0a, -off, 1.0 / std,
                                         Alu.add, Alu.mult)
                    v.drain()
                    it = v.tensor_mul(prfa, prfa, prfa)
                    sm_v.inc(it)
                vmk = sm_v.v

                @block.scalar
                def _(s, h=h):
                    s.wait_ge(sm_v.h, vmk)
                    it = s.activation(probsT[:, h * 384:(h + 1) * 384], prfa,
                                      Act.Exp, bias=0.0, scale=-0.5)
                    sm_s.inc(it)
                smk = sm_s.v

                @block.vector
                def _(v2, h=h, std=std):
                    v2.wait_ge(sm_s.h, smk)
                    inv_n = 1.0 / (std * math.sqrt(2.0 * math.pi))
                    it = v2.tensor_scalar(probsT[:, h * 384:(h + 1) * 384],
                                          probsT[:, h * 384:(h + 1) * 384],
                                          inv_n, None, Alu.mult)
                    sm_v.inc(it)
            pb_done = sm_v.v

            @block.vector
            def _(v):
                v.drain()

            wim_box = []

            @block.sync
            def _(s):
                DmaQ.wait(s, marks["wi_cast"])
                wim_box.append(dqx.dma(
                    s, wio_sb[:], wi_bf.rearrange("(dc p) o -> p dc o", p=128)))

            xl2_done = []
            vsl_ev = []
            vmm_m = []
            vps_ev = []
            ventry = dqx.snapshot()
            for c in range(NCH):
                buf = c % 2

                xm_box = []

                @block.sync
                def _(s):
                    if c < 2:
                        DmaQ.wait(s, ventry)
                    if c >= 2:
                        s.wait_ge(sm_v.h, xl2_done[c - 2])
                    DmaQ.wait(s, xstate_mark[c])
                    xm_box.append(dqx.dma(s, xch[:, buf * 1024:(buf + 1) * 1024],
                                          x_state[c * 128:(c + 1) * 128, :]))

                @block.vector
                def _(v):
                    DmaQ.wait(v, xm_box[0])
                xl2_done.append(
                    emit_ln_chunk(xch[:, buf * 1024:(buf + 1) * 1024],
                                  xl2_ch[:, buf * 1024:(buf + 1) * 1024]))
                lnm = xl2_done[-1]

                @block.tensor
                def _(tn):
                    tn.wait_ge(sm_v.h, lnm)
                    if vsl_ev:
                        tn.wait_ge(sm_v.h, vsl_ev[-1])
                    for dcc in range(DC):
                        it = tn.transpose(psb[:, dcc * 128:(dcc + 1) * 128],
                                          xl2_ch[:, buf * 1024 + dcc * 128:
                                              buf * 1024 + (dcc + 1) * 128],
                                          ident_b[:])
                    sm_t.inc(it)
                tm = sm_t.v

                @block.vector
                def _(v):
                    v.wait_ge(sm_t.h, tm)
                    if len(vmm_m) >= 2:
                        v.wait_ge(sm_t.h, vmm_m[-2])
                    it = v.tensor_copy(tsl[:, buf * 1024:(buf + 1) * 1024],
                                       psb[:, 0:1024])
                    sm_v.inc(it)
                vsl_ev.append(sm_v.v)
                evm = sm_v.v

                @block.tensor
                def _(tn):
                    tn.wait_ge(sm_v.h, evm)
                    DmaQ.wait(tn, wim_box[0])
                    if len(vps_ev) >= 2:
                        tn.wait_ge(sm_v.h, vps_ev[-2])
                    pofs = 1024 + (c % 2) * 1024
                    for nh2 in range(2):
                        for dcc in range(DC):
                            it = tn.matmul(
                                ps[:, pofs + nh2 * 512:pofs + (nh2 + 1) * 512],
                                tsl[:, buf * 1024 + dcc * 128:
                                    buf * 1024 + (dcc + 1) * 128],
                                wio_sb[:, dcc * 1024 + nh2 * 512:
                                       dcc * 1024 + (nh2 + 1) * 512],
                                start=(dcc == 0), stop=(dcc == DC - 1))
                    sm_t.inc(it)
                vmm_m.append(sm_t.v)
                vm = sm_t.v

                @block.vector
                def _(v):
                    v.wait_ge(sm_t.h, vm)
                    pofs = 1024 + (c % 2) * 1024
                    it = v.tensor_copy(vst[:, c * 1024:(c + 1) * 1024],
                                       ps[:, pofs:pofs + 1024])
                    sm_v.inc(it)
                vps_ev.append(sm_v.v)
            v_done = sm_v.v

            wom_box = []

            @block.sync
            def _(s):
                s.wait_ge(sm_t.h, vmm_m[-1])
                DmaQ.wait(s, marks["wo_cast"])
                wom_box.append(dqx.dma(
                    s, wio_sb[:], wo_bf.rearrange("(dc p) o -> p dc o", p=128)))

            att_ev = []
            att_tr = []
            atl_ev = []
            omm_m = []
            ops_ev = []
            oln_done = []
            ocp_ev = []
            store_marks = []
            oentry = dqx.snapshot()
            for c in range(NCH):
                buf = c % 2
                pofs = (c % 2) * 1024

                @block.tensor
                def _(tn):
                    tn.wait_ge(sm_v.h, v_done)
                    if len(att_ev) >= 2:
                        tn.wait_ge(sm_v.h, att_ev[-2])
                    js = [j for j in (-1, 0, 1) if 0 <= c + j < NCH]
                    for hh in range(NH):
                        for ji, j in enumerate(js):
                            it = tn.matmul(
                                ps[:, pofs + hh * 128:pofs + (hh + 1) * 128],
                                probsT[:, (hh * 3 + j + 1) * 128:
                                       (hh * 3 + j + 2) * 128],
                                vst[:, (c + j) * 1024 + hh * 128:
                                    (c + j) * 1024 + (hh + 1) * 128],
                                start=(ji == 0), stop=(ji == len(js) - 1))
                    sm_t.inc(it)
                am = sm_t.v

                @block.vector
                def _(v):
                    v.wait_ge(sm_t.h, am)
                    if len(att_tr) >= 2:
                        v.wait_ge(sm_t.h, att_tr[-2])
                    it = v.tensor_copy(att_bf[:, buf * 1024:(buf + 1) * 1024],
                                       ps[:, pofs:pofs + 1024])
                    sm_v.inc(it)
                att_ev.append(sm_v.v)
                aem = sm_v.v

                @block.tensor
                def _(tn):
                    tn.wait_ge(sm_v.h, aem)
                    if atl_ev:
                        tn.wait_ge(sm_v.h, atl_ev[-1])
                    bofs = 4096 + (c % 2) * 1024
                    for dcc in range(DC):
                        it = tn.transpose(psb[:, bofs + dcc * 128:
                                              bofs + (dcc + 1) * 128],
                                          att_bf[:, buf * 1024 + dcc * 128:
                                              buf * 1024 + (dcc + 1) * 128],
                                          ident_b[:])
                    sm_t.inc(it)
                att_tr.append(sm_t.v)
                atm = sm_t.v

                @block.vector
                def _(v):
                    v.wait_ge(sm_t.h, atm)
                    if len(omm_m) >= 2:
                        v.wait_ge(sm_t.h, omm_m[-2])
                    bofs = 4096 + (c % 2) * 1024
                    it = v.tensor_copy(tsl[:, buf * 1024:(buf + 1) * 1024],
                                       psb[:, bofs:bofs + 1024])
                    sm_v.inc(it)
                atl_ev.append(sm_v.v)
                aslm = sm_v.v

                xm_box = []

                @block.sync
                def _(s):
                    if c < 2:
                        DmaQ.wait(s, oentry)
                        s.wait_ge(sm_v.h, xl2_done[-1])
                    if len(store_marks) >= 2:
                        DmaQ.wait(s, store_marks[-2])
                    if len(ocp_ev) >= 2:
                        s.wait_ge(sm_v.h, ocp_ev[-2])
                    DmaQ.wait(s, xstate_mark[c])
                    xm_box.append(dqx.dma(s, xch[:, buf * 1024:(buf + 1) * 1024],
                                          x_state[c * 128:(c + 1) * 128, :]))

                @block.tensor
                def _(tn):
                    tn.wait_ge(sm_v.h, aslm)
                    DmaQ.wait(tn, wom_box[0])
                    if len(ops_ev) >= 2:
                        tn.wait_ge(sm_v.h, ops_ev[-2])
                    tofs = 2048 + (c % 2) * 1024
                    for nh2 in range(2):
                        for dcc in range(DC):
                            it = tn.matmul(
                                ps[:, tofs + nh2 * 512:tofs + (nh2 + 1) * 512],
                                tsl[:, buf * 1024 + dcc * 128:
                                    buf * 1024 + (dcc + 1) * 128],
                                wio_sb[:, dcc * 1024 + nh2 * 512:
                                       dcc * 1024 + (nh2 + 1) * 512],
                                start=(dcc == 0), stop=(dcc == DC - 1))
                    sm_t.inc(it)
                omm_m.append(sm_t.v)
                om = sm_t.v

                @block.vector
                def _(v):
                    v.wait_ge(sm_t.h, om)
                    DmaQ.wait(v, xm_box[0])
                    tofs = 2048 + (c % 2) * 1024
                    it = v.tensor_add(xch[:, buf * 1024:buf * 1024 + 512],
                                      xch[:, buf * 1024:buf * 1024 + 512],
                                      ps[:, tofs:tofs + 512])
                    it = v.tensor_add(xch[:, buf * 1024 + 512:buf * 1024 + 1024],
                                      xch[:, buf * 1024 + 512:buf * 1024 + 1024],
                                      ps[:, tofs + 512:tofs + 1024])
                    sm_v.inc(it)
                ops_ev.append(sm_v.v)
                adm = sm_v.v

                @block.sync
                def _(s):
                    s.wait_ge(sm_v.h, adm)
                    a_dst = out if cfg.stop_after == "attn" else x_state
                    mk = dqx.dma(s, a_dst[c * 128:(c + 1) * 128, :],
                                 xch[:, buf * 1024:(buf + 1) * 1024])
                    store_marks.append(mk)
                    xstate_mark[c] = mk

                @block.vector
                def _(v):
                    v.drain()
                    it = v.tensor_copy(xc_s[:], xch[:, buf * 1024:(buf + 1) * 1024])
                    sm_v.inc(it)
                ocp_ev.append(sm_v.v)
                oln_done.append(emit_ln_chunk(xc_s[:], xln[:, c * 1024:(c + 1) * 1024]))

        for call in (1, 2):
            sel_es = ExitStack()
            with sel_es:
                aff_T = sel_es.enter_context(
                    nc.sbuf_tensor(f"aff_T{call}", [8, S], F32))
                mask_T = sel_es.enter_context(
                    nc.sbuf_tensor(f"mask_T{call}", [8, S], F32))
                ct32 = sel_es.enter_context(
                    nc.sbuf_tensor(f"ct32_{call}", [8, S], F32))
                bis_junk = sel_es.enter_context(
                    nc.sbuf_tensor(f"bj{call}", [8, S], FP16))
                lo_t = sel_es.enter_context(nc.sbuf_tensor(f"lo{call}", [8, 1], F32))
                gi_t = sel_es.enter_context(nc.sbuf_tensor(f"gi{call}", [8, 1], U8))
                hi_t = sel_es.enter_context(nc.sbuf_tensor(f"hi{call}", [8, 1], F32))
                mid_t = sel_es.enter_context(nc.sbuf_tensor(f"mid{call}", [8, 1], F32))
                cnt_t = sel_es.enter_context(nc.sbuf_tensor(f"cnt{call}", [8, 1], F32))
                ge_t = sel_es.enter_context(nc.sbuf_tensor(f"ge{call}", [8, 1], U8))

                barrier()
                phase_ln_router(call, x_in if call == 1 else None, aff_T)
                barrier()
                sel_done = phase_selection(
                    call, (aff_T, mask_T, ct32, bis_junk, lo_t, hi_t, mid_t,
                           cnt_t, ge_t, gi_t))

            exp_es = ExitStack()
            with exp_es:
                x_inT = exp_es.enter_context(
                    nc.sbuf_tensor(f"x_inT{call}", [128, DC * K], BF16))
                y_pair = exp_es.enter_context(
                    nc.sbuf_tensor(f"y_pair{call}", [128, 2 * RC * 1024], BF16))
                PT_b = exp_es.enter_context(
                    nc.sbuf_tensor(f"PT_b{call}", [128, 2 * 512], BF16))

                st = dict(x_inT=x_inT, y_pair=y_pair, PT_b=PT_b,
                          pt_cons=[], gath_ev=[], down_ev=[],
                          wgu_dma=[], wgu_marks=[], wd_cons=[])
                for pair in range(E // 2):
                    e1, e2 = 2 * pair, 2 * pair + 1
                    barrier()
                    work_es = ExitStack()
                    with work_es:
                        st["hT"] = work_es.enter_context(
                            nc.sbuf_tensor(f"hT{call}_{pair}", [128, NICH * RH],
                                           BF16))
                        st["wgu_sl"] = work_es.enter_context(
                            nc.sbuf_tensor(f"wgu{call}_{pair}", [128, 2 * 4096],
                                           BF16))
                        st["wd_sl"] = work_es.enter_context(
                            nc.sbuf_tensor(f"wd{call}_{pair}", [128, 2 * 2048],
                                           BF16))
                        phase_expert(call, e1, 0, st, sel_done)
                        phase_expert(call, e2, 1, st, sel_done)
                        y_ready = sm_v.v
                    barrier()
                    sc_es = ExitStack()
                    with sc_es:
                        cbc = sc_es.enter_context(
                            nc.sbuf_tensor(f"cbc{call}_{pair}", [128, 2 * S], FP16))
                        P_b = sc_es.enter_context(
                            nc.sbuf_tensor(f"P_b{call}_{pair}", [128, 16 * 256],
                                           BF16))
                        cm1 = emit_cbc(e1, 0, cbc, y_ready, sel_done)
                        cbc_ready = emit_cbc(e2, 1, cbc, y_ready, sel_done,
                                             after=cm1)
                        psrc = (x_in if (call == 1 and pair == 0) else x_state)
                        last = (pair == E // 2 - 1)
                        stop1 = (cfg.stop_after == "ffn1" and call == 1)
                        pdst = out if (last and (call == 2 or stop1)) else x_state
                        phase_scatter(call, psrc, pdst, not (call == 2 and last),
                                      cbc, P_b, x_inT[:, 0:2 * RC * 256], y_pair,
                                      cbc_ready, y_ready)

            if cfg.stop_after == "ffn1":
                break
            if call == 1:
                barrier()
                att_es = ExitStack()
                with att_es:
                    att = dict(
                        vst=att_es.enter_context(
                            nc.sbuf_tensor("vst", [128, NCH * 1024], BF16)),
                        wio_sb=att_es.enter_context(
                            nc.sbuf_tensor("wio_sb", [128, DC * 1024], BF16)),
                        xl2_ch=att_es.enter_context(
                            nc.sbuf_tensor("xl2_ch", [128, 2 * 1024], BF16)),
                        xc_s=att_es.enter_context(
                            nc.sbuf_tensor("xc_s", [128, 1024], F32)),
                        probsT=att_es.enter_context(
                            nc.sbuf_tensor("probsT", [128, NH * 3 * 128], BF16)))
                    att_bf = att_es.enter_context(
                        nc.sbuf_tensor("att_bf", [128, 2 * 1024], BF16))
                    phase_attention(att)
                barrier()
            if cfg.stop_after == "attn":
                break

        @block.sync
        def _(s):
            dqx.wait_all(s)
            dqg.wait_all(s)

    return nc


def kernel(L0_attn_values, L0_attn_input_weights, L0_attn_output_weight,
           L0_ffn_gate_w, L0_ffn_w_gate, L0_ffn_w_up, L0_ffn_w_down):
    cfg = Cfg()
    B = L0_attn_values.shape[0]
    x = np.ascontiguousarray(np.asarray(L0_attn_values, dtype=np.float32))
    wgT = np.ascontiguousarray(np.asarray(L0_ffn_w_gate, np.float32).transpose(0, 2, 1))
    wuT = np.ascontiguousarray(np.asarray(L0_ffn_w_up, np.float32).transpose(0, 2, 1))
    wdT = np.ascontiguousarray(np.asarray(L0_ffn_w_down, np.float32).transpose(0, 2, 1))
    gwT = np.ascontiguousarray(np.asarray(L0_ffn_gate_w, np.float32).T)
    wiT = np.ascontiguousarray(np.asarray(L0_attn_input_weights, np.float32).T)
    woT = np.ascontiguousarray(np.asarray(L0_attn_output_weight, np.float32).T)

    nc = build_program(cfg)
    in_maps = []
    for b in range(B):
        in_maps.append(dict(x_in=np.ascontiguousarray(x[b]), wgT=wgT, wuT=wuT,
                            wdT=wdT, gwT=gwT, wiT=wiT, woT=woT))
    res = run_bass_kernel_spmd(nc, in_maps, core_ids=list(range(B)))
    return np.stack([res.results[b]["out"] for b in range(B)], axis=0)
